# version 03 (final)
# baseline (speedup 1.0000x reference)
import contextlib
import numpy as np
import ml_dtypes
import concourse.bass as bass
import concourse.mybir as mybir
from concourse.bass_utils import run_bass_kernel_spmd

F32 = mybir.dt.float32
BF16 = mybir.dt.bfloat16
AF = mybir.ActivationFunctionType
ALU = mybir.AluOpType
AX = mybir.AxisListType

D = 1024
H = 16
HD = 64
NFC = 8
VSTRIDE = H * 65 + 2

ENGINES = ("pe", "act", "dve", "pool", "sp")
BLOCK_METHOD = {"pe": "tensor", "act": "scalar", "dve": "vector",
                "pool": "gpsimd", "sp": "sync"}
DMA_QUEUES = {"qsp": "sp", "qact": "act", "qpool": "pool"}
BUCKET = 2048



_DTSIZE = {F32: 4, BF16: 2, mybir.dt.int32: 4, mybir.dt.float16: 2}


def _bbox(ap):
    t = ap.tensor
    space = str(t.space).upper()
    key = f"{space}:{t.name}"
    dims = ap.ap
    off = ap.offset
    esz = _DTSIZE.get(ap.dtype, 4)
    if "PSUM" not in space and ("DRAM" in space or not dims):
        span = 1
        for st, ct in dims:
            span += abs(st) * (ct - 1)
        return (key, 0, 128, off * esz, (off + span) * esz)
    rowstride = dims[0][0]
    pcnt = dims[0][1]
    if rowstride > 0:
        p0 = off // rowstride
        flo = off - p0 * rowstride
    else:
        p0 = 0
        flo = off
    span = 1
    for st, ct in dims[1:]:
        span += abs(st) * (ct - 1)
    fhi = (flo + span) * esz
    flo = flo * esz
    if "PSUM" in space:
        flo = (flo // 2048) * 2048
        fhi = ((fhi + 2047) // 2048) * 2048
        return (key, 0, 128, flo, fhi)
    return (key, p0, p0 + pcnt, flo, fhi)


def _overlap(a, b):
    return (a[1] < b[2] and b[1] < a[2] and a[3] < b[4] and b[3] < a[4])


def _covers(a, b):
    return (a[1] <= b[1] and a[2] >= b[2] and a[3] <= b[3] and a[4] >= b[4])


class Op:
    __slots__ = ("idx", "engine", "fn", "queue", "tag", "sig", "waits", "slot")

    def __init__(self, idx, engine, fn, queue, tag):
        self.idx = idx
        self.engine = engine
        self.fn = fn
        self.queue = queue
        self.tag = tag
        self.sig = None
        self.waits = {}
        self.slot = None


class Prog:
    SELF_SYNC = {"pool", "dve", "act", "sp"}
    NSLOT = 8

    def __init__(self, nc):
        self.nc = nc
        self.ops = []
        self.log = {}
        self.need_sig = set()
        self.qn = {q: 0 for q in DMA_QUEUES}
        self.slot_prev = {}

    def op(self, engine, fn, reads=(), writes=(), tag=""):
        return self._add(engine, fn, reads, writes, None, tag)

    def dma(self, queue, out_ap, in_ap, tag="", **kw):
        eng = DMA_QUEUES[queue]

        def fn(e, _o=out_ap, _i=in_ap, _kw=kw):
            return e.dma_start(out=_o, in_=_i, **_kw)
        return self._add(eng, fn, [in_ap], [out_ap], queue, tag)

    def dmat(self, queue, out_ap, in_ap, tag=""):
        eng = DMA_QUEUES[queue]

        def fn(e, _o=out_ap, _i=in_ap):
            return e.dma_start_transpose(_o, _i)
        return self._add(eng, fn, [in_ap], [out_ap], queue, tag)

    def _buckets(self, box):
        if "DRAM" in box[0]:
            return ((box[0], 0),)
        return [(box[0], b) for b in range(box[3] // BUCKET, box[4] // BUCKET + 1)]

    def _add(self, engine, fn, reads, writes, queue, tag):
        idx = len(self.ops)
        rb = [_bbox(a) if isinstance(a, bass.AP) else a for a in reads]
        wb = [_bbox(a) if isinstance(a, bass.AP) else a for a in writes]
        o = Op(idx, engine, fn, queue, tag)
        if queue is not None:
            o.slot = (queue, self.qn[queue] % self.NSLOT)
            self.qn[queue] += 1
        deps = set()
        for b in rb:
            bk = b[1:]
            for k in self._buckets(b):
                d = self.log.get(k)
                if not d:
                    continue
                for bx, ent in d.items():
                    if (b[1] < bx[1] and bx[0] < b[2] and b[3] < bx[3]
                            and bx[2] < b[4] and ent[0] is not None):
                        deps.add(ent[0])
        for b in wb:
            bk = b[1:]
            for k in self._buckets(b):
                d = self.log.get(k)
                if not d:
                    continue
                for bx, ent in d.items():
                    if (b[1] < bx[1] and bx[0] < b[2] and b[3] < bx[3]
                            and bx[2] < b[4]):
                        if ent[0] is not None:
                            deps.add(ent[0])
                        deps.update(ent[1])
        if queue is not None:
            prev = self.slot_prev.get(o.slot)
            if prev is not None:
                deps.add(prev)
            self.slot_prev[o.slot] = idx
        waits = {}
        for pidx in deps:
            p = self.ops[pidx]
            if p.queue is None:
                if p.engine == engine and engine not in self.SELF_SYNC:
                    continue
                key = p.engine
            else:
                key = p.slot
            cur = waits.get(key)
            if cur is None or pidx > cur:
                waits[key] = pidx
        o.waits = waits
        self.need_sig.update(waits.values())
        for b in rb:
            bk = (b[1], b[2], b[3], b[4])
            for k in self._buckets(b):
                d = self.log.setdefault(k, {})
                ent = d.get(bk)
                if ent is None:
                    d[bk] = [None, [idx]]
                else:
                    ent[1].append(idx)
        for b in wb:
            bk = (b[1], b[2], b[3], b[4])
            for k in self._buckets(b):
                d = self.log.setdefault(k, {})
                dead = [bx for bx in d
                        if bx != bk and b[1] <= bx[0] and b[2] >= bx[1]
                        and b[3] <= bx[2] and b[4] >= bx[3]]
                for bx in dead:
                    del d[bx]
                d[bk] = [idx, []]
        self.ops.append(o)
        return o

    def build(self, es):
        nc = self.nc
        sems = {}
        for e in ENGINES:
            sems[e] = es.enter_context(nc.semaphore(f"sem_{e}"))
        for q in DMA_QUEUES:
            for s in range(self.NSLOT):
                sems[(q, s)] = es.enter_context(nc.semaphore(f"sem_{q}{s}"))
        sig_no = {}
        counters = {e: 0 for e in ENGINES}
        scount = {}
        for o in self.ops:
            if o.queue is not None:
                scount[o.slot] = scount.get(o.slot, 0) + 1
                sig_no[o.idx] = scount[o.slot] * 16
            elif o.idx in self.need_sig:
                counters[o.engine] += 1
                sig_no[o.idx] = counters[o.engine]
                o.sig = counters[o.engine]
        self.final_q = dict(scount)
        ops_by_engine = {e: [] for e in ENGINES}
        for o in self.ops:
            o.waits = {k: sig_no[p] for k, p in o.waits.items()}
            ops_by_engine[o.engine].append(o)
        hw = {e: {} for e in ENGINES}
        for e in ENGINES:
            for o in ops_by_engine[e]:
                nw = {}
                for key, val in o.waits.items():
                    if hw[e].get(key, 0) < val:
                        nw[key] = val
                        hw[e][key] = val
                o.waits = nw
        with nc.Block() as block:
            for e in ENGINES:
                olist = ops_by_engine[e]

                def body(eng, _olist=olist, _e=e):
                    for o in _olist:
                        for key, val in sorted(o.waits.items(), key=str):
                            eng.wait_ge(sems[key], val)
                        inst = o.fn(eng)
                        if o.queue is not None:
                            inst.then_inc(sems[o.slot], 16)
                        elif o.sig is not None:
                            inst.then_inc(sems[_e], 1)
                    for slot, n in self.final_q.items():
                        if DMA_QUEUES[slot[0]] == _e and n > 0:
                            eng.wait_ge(sems[slot], n * 16)

                getattr(block, BLOCK_METHOD[e])(body)



class Ctx:
    pass


def mk_ctx(nc, P, T):
    C = Ctx()
    C.nc = nc
    C.P = P
    C.T = T
    C.NT = T // 128
    C.NTB = T // 512
    a = nc.alloc_sbuf_tensor
    C.BIGK = a("BIGK", [128, NFC * T], BF16)
    C.BIGV = a("BIGV", [128, max((T // 128) * VSTRIDE, 48 * 512)], BF16)
    C.WBUF = a("WBUF", [128, 12288], BF16)
    C.XBUF = a("XBUF", [128, 1024], F32)
    C.SCR = a("SCR", [128, 4096], F32)
    C.SCRB = a("SCRB", [128, 9600], BF16)
    C.STAT = a("STAT", [128, 128], F32)
    C.VECROW = a("VECROW", [128, 2432], F32)
    C.CONSTF = a("CONSTF", [128, 160], F32)
    C.IDF = a("IDF", [128, 128], F32)
    C.IDB = a("IDB", [128, 128], BF16)
    C.MASKQ = a("MASKQ", [128, 1024], F32)
    C.PAIR0 = nc.alloc_psum_tensor("PAIR0", [128, 1024], F32)
    C.PAIR1 = nc.alloc_psum_tensor("PAIR1", [128, 1024], F32)
    C.PAIR2 = nc.alloc_psum_tensor("PAIR2", [128, 1024], F32)
    C.PAIR3 = nc.alloc_psum_tensor("PAIR3", [128, 1024], F32)
    C.PS6 = C.PAIR3
    return C


XN_OFF = 0
XNFM_OFF = 4096
STG_OFF = 8192
QQB_OFF = 0
CTXB_OFF = 4096
PT_OFF = 8192

LN_EPS = 1e-5


def const_col(C, value):
    if value in C._consts:
        return C._consts[value]
    i = C._const_next
    C._const_next += 1
    assert i < 136
    col = C.CONSTF[:, i:i + 1]
    C.P.op("dve", lambda e: e.memset(col, float(value)), writes=[col],
           tag=f"const_{value}")
    C._consts[value] = col
    return col


def emit_consts(C):
    P = C.P
    C._consts = {}
    C._const_next = 130
    P.op("dve", lambda e: e.memset(C.CONSTF[:, :], 1.0),
         writes=[C.CONSTF[:, :]], tag="const_ones")
    for t in (C.IDF, C.IDB):
        P.op("pool", lambda e, _t=t: e.memset(_t[:, :], 1.0), writes=[t[:, :]])
        P.op("pool", lambda e, _t=t: e.affine_select(
            _t[:, :], _t[:, :], [[-1, 128]], ALU.is_equal, 0.0,
            base=0, channel_multiplier=1),
            reads=[t[:, :]], writes=[t[:, :]], tag="ident")


def ones_row(C, n):
    return C.CONSTF[0:1, 0:n]


def col_to_row(C, col_ap, out_row, scale=1.0):
    P = C.P
    n = col_ap.ap[0][1]
    ps = C.PS6[0:1, 0:n]
    ident = C.IDF[0:n, 0:n]
    P.op("pe", lambda e: e.transpose(ps, col_ap, ident),
         reads=[col_ap, C.IDF[:, :]], writes=[ps], tag="c2r")
    P.op("act", lambda e: e.activation(out_row, ps, AF.Copy, scale=scale),
         reads=[ps], writes=[out_row], tag="c2r_ev")


def bcast_row(C, row_ap, nrows, psum_slice):
    P = C.P
    P.op("pe", lambda e: e.matmul(psum_slice, ones_row(C, nrows), row_ap,
                                  start=True, stop=True),
         reads=[row_ap, ones_row(C, nrows)], writes=[psum_slice], tag="bcast")



class WSpec:

    def __init__(self, C, name, src_ap, rows, cols, mode, cast_pairs=None):
        self.C = C
        self.name = name
        self.mode = mode
        self.rows = rows
        self.cols = cols
        self.cast = C.nc.dram_tensor(f"wc_{name}", [rows, cols], BF16,
                                     kind="Internal").ap()
        if cast_pairs is None:
            cast_pairs = [(self.cast[:, :], src_ap)]
        else:
            cast_pairs = [(self.cast[r0:r1, c0:c1], s)
                          for (r0, r1, c0, c1, s) in cast_pairs]
        for dst, s in cast_pairs:
            C.P.dma("qpool", dst, s, tag=f"cast_{name}")
        if mode == "trans":
            P = C.P
            self.wt = C.nc.dram_tensor(f"wt_{name}", [cols, rows], BF16,
                                       kind="Internal").ap()
            stg = C.SCRB[:, STG_OFF:STG_OFF + 1024]
            for c0 in range(0, cols, 128):
                for r0 in range(0, rows, 1024):
                    r1 = min(r0 + 1024, rows)
                    P.dmat("qact", stg[:, 0:r1 - r0],
                           self.cast[r0:r1, c0:c0 + 128], tag=f"pt_{name}")
                    P.dma("qact", self.wt[c0:c0 + 128, r0:r1],
                          stg[:, 0:r1 - r0], tag=f"ptw_{name}")

    def load_slab(self, dst_ap, c0, c1, o0, o1, queue="qact"):
        if self.mode == "direct":
            self.C.P.dma(queue, dst_ap, self.cast[c0:c1, o0:o1],
                         tag=f"w_{self.name}")
        else:
            self.C.P.dma(queue, dst_ap, self.wt[c0:c1, o0:o1],
                         tag=f"wl_{self.name}")



def ln_tile(C, x_src_ap, ti, xbuf_half, out_bf16_ap, eps=LN_EPS):
    P = C.P
    xb = C.XBUF[:, 0:1024]
    P.dma("qsp", xb, x_src_ap, tag=f"xload")
    ci = ti % 32
    negmu = C.STAT[:, ci:ci + 1]
    rstd = C.STAT[:, 32 + ci:33 + ci]
    tmp = C.STAT[:, 64:65]
    P.op("dve", lambda e: e.tensor_reduce(negmu, xb, AX.X, ALU.add),
         reads=[xb], writes=[negmu], tag="ln_sum")
    P.op("dve", lambda e: e.tensor_scalar_mul(negmu, negmu, -1.0 / D),
         reads=[negmu], writes=[negmu], tag="ln_negmu")
    sq = C.SCR[:, 0:1024]
    P.op("act", lambda e: e.activation(sq, xb, AF.Square, bias=negmu,
                                       accum_out=tmp),
         reads=[xb, negmu], writes=[sq, tmp], tag="ln_sq")
    epsc = const_col(C, eps)
    P.op("act", lambda e: e.activation(tmp, tmp, AF.Sqrt, scale=1.0 / D,
                                       bias=epsc),
         reads=[tmp, epsc], writes=[tmp], tag="ln_sd")
    P.op("dve", lambda e: e.reciprocal(rstd, tmp),
         reads=[tmp], writes=[rstd], tag="ln_rstd")
    P.op("dve", lambda e: e.tensor_scalar(out_bf16_ap, xb, negmu, rstd,
                                          ALU.add, ALU.mult),
         reads=[xb, negmu, rstd], writes=[out_bf16_ap], tag="ln_xn")
    return negmu, rstd, xb


def transpose_to_fm(C, src_bf16_1024, tt, dsts):
    P = C.P
    for dc in range(NFC):
        pt = C.PAIR2[:, (dc % 2) * 512:(dc % 2) * 512 + 64].bitcast(BF16)
        s = bass.AP(src_bf16_1024.tensor, src_bf16_1024.offset + dc * 128,
                    [[src_bf16_1024.ap[0][0], 128], [1, 128]])
        P.op("pe", lambda e, _p=pt, _s=s: e.transpose(_p, _s, C.IDB[:, :]),
             reads=[s, C.IDB[:, :]], writes=[pt], tag="t2fm")
        dst = dsts[dc]
        P.op("dve", lambda e, _d=dst, _p=pt: e.tensor_copy(_d, _p),
             reads=[pt], writes=[dst], tag="t2fm_ev")


def xnfm_dsts(C, tt):
    return [C.SCRB[:, XNFM_OFF + dc * 512 + tt * 128:
                   XNFM_OFF + dc * 512 + (tt + 1) * 128] for dc in range(NFC)]


def fm_proj(C, wbuf_off, fc, rhs_off, psum, bias_col=None, evac_scale=1.0,
            dst=None, tag="proj"):
    P = C.P
    for dc in range(NFC):
        lhsT = C.WBUF[:, wbuf_off + dc * 1024 + fc * 128:
                      wbuf_off + dc * 1024 + (fc + 1) * 128]
        rhs = C.SCRB[:, rhs_off + dc * 512:rhs_off + (dc + 1) * 512]
        P.op("pe", lambda e, _o=psum, _l=lhsT, _r=rhs, _s=(dc == 0),
             _e=(dc == NFC - 1): e.matmul(_o, _l, _r, start=_s, stop=_e),
             reads=[lhsT, rhs], writes=[psum], tag=tag)
    if dst is None:
        return
    if bias_col is not None:
        P.op("dve", lambda e: e.tensor_scalar(dst, psum, evac_scale, bias_col,
                                              ALU.mult, ALU.add),
             reads=[psum, bias_col], writes=[dst], tag=tag + "_ev")
    elif evac_scale != 1.0:
        P.op("dve", lambda e: e.tensor_scalar_mul(dst, psum, evac_scale),
             reads=[psum], writes=[dst], tag=tag + "_ev")
    else:
        P.op("dve", lambda e: e.tensor_copy(dst, psum),
             reads=[psum], writes=[dst], tag=tag + "_ev")


def attn_phase_A(C, blk):
    P = C.P
    T = C.T
    qk_custom = getattr(blk, "qk_custom", None)
    w_post = getattr(blk, "w_post_hook", None)
    blk.load_w(C, "q")
    if w_post is not None:
        w_post(C, "q")
    for tb in range(C.NTB):
        for tt in range(4):
            ti = tb * 4 + tt
            xn = C.SCRB[:, XN_OFF + tt * 1024:XN_OFF + (tt + 1) * 1024]
            negmu, rstd, xb = ln_tile(C, blk.x_in[ti * 128:(ti + 1) * 128, :],
                                      ti, tt % 2, xn)
            if blk.pre_ln_hook is not None:
                blk.pre_ln_hook(C, blk, ti, xb, negmu, rstd, xn)
            transpose_to_fm(C, xn, tt, xnfm_dsts(C, tt))
        for dc in range(NFC):
            src = C.SCRB[:, XNFM_OFF + dc * 512:XNFM_OFF + (dc + 1) * 512]
            P.dma("qsp", blk.xfscr[dc * 128:(dc + 1) * 128,
                                   tb * 512:(tb + 1) * 512], src, tag="xfscr_w")
        if qk_custom is not None:
            qk_custom(C, blk, "q", tb)
            continue
        for fc in range(NFC):
            ps = (C.PAIR0 if fc % 2 == 0 else C.PAIR1)[:, 0:512]
            stg = C.SCRB[:, STG_OFF:STG_OFF + 512]
            bcol = blk.q_bias[:, fc:fc + 1] if blk.q_bias is not None else None
            fm_proj(C, 0, fc, XNFM_OFF, ps, bcol, blk.q_evac_scale, stg, "qproj")
            P.dma("qsp", blk.qscr[fc * 128:(fc + 1) * 128,
                                  tb * 512:(tb + 1) * 512], stg, tag="qscr_w")
            if blk.qn_hook is not None:
                blk.qn_hook(C, blk, fc, tb, stg)
    blk.load_w(C, "k")
    if w_post is not None:
        w_post(C, "k")
    for tb in range(C.NTB):
        for dc in range(NFC):
            dst = C.SCRB[:, XNFM_OFF + dc * 512:XNFM_OFF + (dc + 1) * 512]
            P.dma("qsp", dst, blk.xfscr[dc * 128:(dc + 1) * 128,
                                        tb * 512:(tb + 1) * 512], tag="xfscr_r")
        if qk_custom is not None:
            qk_custom(C, blk, "k", tb)
            continue
        for fc in range(NFC):
            ps = (C.PAIR0 if fc % 2 == 0 else C.PAIR1)[:, 0:512]
            dst = C.BIGK[:, fc * T + tb * 512:fc * T + (tb + 1) * 512]
            bcol = blk.k_bias[:, fc:fc + 1] if blk.k_bias is not None else None
            fm_proj(C, 0, fc, XNFM_OFF, ps, bcol, 1.0, dst, "kproj")
            if blk.kn_hook is not None:
                blk.kn_hook(C, blk, fc, tb, dst)
    blk.load_w(C, "v")
    if w_post is not None:
        w_post(C, "v")
    va = C.BIGV
    vrs = va.ap().ap[0][0]
    ones_ap = bass.AP(va, 64, [[vrs, 128], [VSTRIDE, C.NT], [65, H], [1, 1]])
    P.op("pool", lambda e: e.memset(ones_ap, 1.0), writes=[ones_ap], tag="v_ones")
    for tb in range(C.NTB):
        for dc in range(NFC):
            dst = C.SCRB[:, XNFM_OFF + dc * 512:XNFM_OFF + (dc + 1) * 512]
            P.dma("qsp", dst, blk.xfscr[dc * 128:(dc + 1) * 128,
                                        tb * 512:(tb + 1) * 512], tag="xfscr_r2")
        for tt in range(4):
            jt = tb * 4 + tt
            for half in range(2):
                ps = (C.PAIR0 if half == 0 else C.PAIR1)[:, 0:512]
                for dc in range(NFC):
                    lhsT = C.SCRB[:, XNFM_OFF + dc * 512 + tt * 128:
                                  XNFM_OFF + dc * 512 + (tt + 1) * 128]
                    rhs = C.WBUF[:, dc * 1024 + half * 512:
                                  dc * 1024 + (half + 1) * 512]
                    P.op("pe", lambda e, _o=ps, _l=lhsT, _r=rhs, _s=(dc == 0),
                         _e=(dc == NFC - 1): e.matmul(_o, _l, _r, start=_s,
                                                      stop=_e),
                         reads=[lhsT, rhs], writes=[ps], tag="vproj")
                base = jt * VSTRIDE + half * 8 * 65
                dst = bass.AP(va, base, [[vrs, 128], [65, 8], [1, 64]])
                src = bass.AP(ps.tensor, ps.offset,
                              [[ps.ap[0][0], 128], [64, 8], [1, 64]])
                P.op("dve", lambda e, _d=dst, _s=src: e.tensor_copy(_d, _s),
                     reads=[ps], writes=[dst], tag="v_ev")



def attn_phase_B(C, blk):
    P = C.P
    blk.load_wo(C)
    for qb in range(C.NTB):
        for fc in range(NFC):
            dst = C.SCRB[:, QQB_OFF + fc * 512:QQB_OFF + (fc + 1) * 512]
            P.dma("qsp", dst, blk.qscr[fc * 128:(fc + 1) * 128,
                                       qb * 512:(qb + 1) * 512], tag="qqb")
        if blk.variant in ("std", "l2"):
            _stats_pass(C, blk, qb)
        _softmax_av(C, blk, qb)
        _out_proj(C, blk, qb)


def _stats_pass(C, blk, qb):
    P = C.P
    T = C.T
    for hp in range(8):
        for qt in range(4):
            qtg = qb * 4 + qt
            lhs_full = C.SCRB[:, QQB_OFF + hp * 512 + qt * 128:
                              QQB_OFF + hp * 512 + (qt + 1) * 128]
            nj = (qtg * 128 // 512 + 1) if blk.causal else C.NTB
            for jb in range(nj):
                for hh in range(2):
                    bank = C.PAIR2 if hh == 0 else C.PAIR3
                    ps = bank[:, (jb % 2) * 512:(jb % 2) * 512 + 512]
                    lhsT = lhs_full[hh * 64:(hh + 1) * 64, :]
                    rhs = C.BIGK[hh * 64:(hh + 1) * 64,
                                 hp * T + jb * 512:hp * T + (jb + 1) * 512]
                    tp = (0, 0) if hh == 0 else (64, 0)
                    P.op("pe", lambda e, _o=ps, _l=lhsT, _r=rhs, _t=tp:
                         e.matmul(_o, _l, _r, start=True, stop=True,
                                  tile_position=_t),
                         reads=[lhsT, rhs], writes=[ps], tag="stat_mm")
                    if blk.variant == "l2" and blk.causal and jb == qtg // 4:
                        dd = 128 * (qtg % 4)
                        msk = C.VECROW[:, 512 - dd:1024 - dd]
                        P.op("dve", lambda e, _p=ps, _m=msk: e.tensor_tensor(
                            _p, _p, _m, ALU.min),
                            reads=[ps, msk], writes=[ps], tag="l2mask")
                    if jb % 2 == 1 or jb == nj - 1:
                        wid = 1024 if jb % 2 == 1 else 512
                        src = bank[:, 0:wid]
                        dstp = C.SCR[:, 2112 + hh * 8 + jb // 2:
                                     2113 + hh * 8 + jb // 2]
                        P.op("dve", lambda e, _d=dstp, _s=src: e.tensor_reduce(
                            _d, _s, AX.X, ALU.max),
                            reads=[src], writes=[dstp], tag="stat_max")
            for hh in range(2):
                h = 2 * hp + hh
                parts = C.SCR[:, 2112 + hh * 8:2112 + hh * 8 + (nj + 1) // 2]
                mcol = C.SCR[:, 2048 + h * 4 + qt:2049 + h * 4 + qt]
                P.op("dve", lambda e, _m=mcol, _p=parts: e.tensor_reduce(
                    _m, _p, AX.X, ALU.max), reads=[parts], writes=[mcol],
                    tag="stat_fin")
                if blk.variant == "l2":
                    P.op("dve", lambda e, _m=mcol: e.tensor_scalar(
                        _m, _m, 100.0, -100.0, ALU.min, ALU.max),
                        reads=[mcol], writes=[mcol], tag="l2clipm")


def _negm_row(C, blk, qb, h):
    cb = 3072 if h % 2 == 0 else 3584
    row = C.SCR[0:1, cb:cb + 512]
    for qt in range(4):
        mcol = C.SCR[:, 2048 + h * 4 + qt:2049 + h * 4 + qt]
        col_to_row(C, mcol, C.SCR[0:1, cb + qt * 128:cb + (qt + 1) * 128],
                   scale=-1.0)
    return row


def _softmax_av(C, blk, qb):
    P = C.P
    T = C.T
    njt = (qb + 1) * 4 if blk.causal else C.NT
    vrs = C.BIGV.ap().ap[0][0]
    for h in range(H):
        fc = h // 2
        hh = h % 2
        if blk.variant in ("std", "l2"):
            negm = _negm_row(C, blk, qb, h)
        ctx = C.PAIR1[0:65, (h % 2) * 512:(h % 2) * 512 + 512]
        rhs = C.SCRB[hh * 64:(hh + 1) * 64,
                     QQB_OFF + fc * 512:QQB_OFF + (fc + 1) * 512]
        if blk.variant == "l2":
            for jt in range(njt):
                jj = jt % 2
                ps = C.PAIR0[:, jj * 512:(jj + 1) * 512]
                lhsT = C.BIGK[hh * 64:(hh + 1) * 64,
                              fc * T + jt * 128:fc * T + (jt + 1) * 128]
                P.op("pe", lambda e, _o=ps, _l=lhsT, _r=rhs: e.matmul(
                    _o, _l, _r, start=True, stop=True),
                    reads=[lhsT, rhs], writes=[ps], tag="sT_mm")
                pt = C.SCRB[:, PT_OFF + jj * 512:PT_OFF + (jj + 1) * 512]
                blk.l2_exp(C, blk, qb, h, jt, ps, pt)
                lhsV = bass.AP(C.BIGV, jt * VSTRIDE + h * 65,
                               [[vrs, 128], [1, 65]])
                P.op("pe", lambda e, _c=ctx, _v=lhsV, _t=pt, _s=(jt == 0),
                     _e=(jt == njt - 1): e.matmul(_c, _v, _t, start=_s,
                                                  stop=_e),
                     reads=[lhsV, pt], writes=[ctx], tag="av_mm")
        else:
            for jp in range(0, njt, 2):
                st = (jp // 2) % 2
                bank = C.PAIR0 if st == 0 else C.PAIR2
                for i in range(2):
                    jt = jp + i
                    ps = bank[:, i * 512:(i + 1) * 512]
                    if blk.variant == "std":
                        P.op("pe", lambda e, _o=ps, _m=negm: e.matmul(
                            _o, ones_row(C, 128), _m, start=True, stop=False),
                            reads=[negm, ones_row(C, 128)], writes=[ps],
                            tag="sm_init")
                    lhsT = C.BIGK[hh * 64:(hh + 1) * 64,
                                  fc * T + jt * 128:fc * T + (jt + 1) * 128]
                    stt = blk.variant != "std"
                    P.op("pe", lambda e, _o=ps, _l=lhsT, _r=rhs, _s=stt:
                         e.matmul(_o, _l, _r, start=_s, stop=True),
                         reads=[lhsT, rhs], writes=[ps], tag="sT_mm")
                ptf = (C.SCRB[:, PT_OFF:PT_OFF + 1024] if st == 0
                       else C.MASKQ[:, 0:512].bitcast(BF16))
                if blk.variant == "std":
                    P.op("act", lambda e, _t=ptf, _p=bank[:, 0:1024]:
                         e.activation(_t, _p, AF.Exp, scale=blk.logit_scale),
                         reads=[bank[:, 0:1024]], writes=[ptf], tag="sm_exp")
                else:
                    for i in range(2):
                        pth = bass.AP(ptf.tensor, ptf.offset + i * 512,
                                      [[ptf.ap[0][0], 128], [1, 512]])
                        blk.l3_exp(C, blk, h, jp + i,
                                   bank[:, i * 512:(i + 1) * 512], pth)
                for i in range(2):
                    jt = jp + i
                    pth = bass.AP(ptf.tensor, ptf.offset + i * 512,
                                  [[ptf.ap[0][0], 128], [1, 512]])
                    if blk.causal and jt >= qb * 4:
                        d = qb * 512 - jt * 128
                        P.op("pool", lambda e, _t=pth, _d=d: e.affine_select(
                            _t, _t, [[1, 512]], ALU.is_ge, 0.0, base=_d,
                            channel_multiplier=-1),
                            reads=[pth], writes=[pth], tag="causal0")
                    lhsV = bass.AP(C.BIGV, jt * VSTRIDE + h * 65,
                                   [[vrs, 128], [1, 65]])
                    P.op("pe", lambda e, _c=ctx, _v=lhsV, _t=pth,
                         _s=(jt == 0), _e=(jt == njt - 1):
                         e.matmul(_c, _v, _t, start=_s, stop=_e),
                         reads=[lhsV, pth], writes=[ctx], tag="av_mm")
        blk.ctx_evac(C, blk, qb, h, ctx)


def std_ctx_evac(C, blk, qb, h, ctx):
    P = C.P
    srow = C.SCR[32:33, 512:1024]
    rrow = C.SCR[0:1, 512:1024]
    P.op("dve", lambda e: e.tensor_copy(srow, ctx[64:65, 0:512]),
         reads=[ctx[64:65, 0:512]], writes=[srow], tag="srow")
    P.op("dve", lambda e: e.reciprocal(rrow, srow),
         reads=[srow], writes=[rrow], tag="rrow")
    bc = C.PS6[0:64, 0:512]
    bcast_row(C, rrow, 64, bc)
    rb = C.SCR[0:64, 0:512]
    P.op("dve", lambda e: e.tensor_copy(rb, bc), reads=[bc], writes=[rb],
         tag="rb_ev")
    fc = h // 2
    hh = h % 2
    dst = C.SCRB[hh * 64:(hh + 1) * 64,
                 CTXB_OFF + fc * 512:CTXB_OFF + (fc + 1) * 512]
    cslice = ctx[0:64, 0:512]
    if blk.v_bias is not None:
        tmp = C.SCR[64:128, 0:512]
        P.op("dve", lambda e: e.tensor_mul(tmp, cslice, rb),
             reads=[cslice, rb], writes=[tmp], tag="ctx_mul")
        bcol = blk.v_bias[hh * 64:(hh + 1) * 64, fc:fc + 1]
        P.op("dve", lambda e, _b=bcol: e.tensor_scalar(
            dst, tmp, 1.0, _b, ALU.mult, ALU.add),
            reads=[tmp, bcol], writes=[dst], tag="ctx_bias")
    else:
        P.op("dve", lambda e: e.tensor_mul(dst, cslice, rb),
             reads=[cslice, rb], writes=[dst], tag="ctx_mul")


def _out_proj(C, blk, qb):
    P = C.P
    d0, d1 = blk.out_dchunks
    for qt in range(4):
        qtg = qb * 4 + qt
        pslist = []
        for i, half in enumerate(range(d0, d1)):
            ps = C.PAIR2[:, (i % 2) * 512:(i % 2 + 1) * 512]
            pslist.append(ps)
            for fc in range(NFC):
                lhsT = C.SCRB[:, CTXB_OFF + fc * 512 + qt * 128:
                              CTXB_OFF + fc * 512 + (qt + 1) * 128]
                rhs = C.WBUF[:, blk.wo_off + fc * 1024 + (half - d0) * 512:
                             blk.wo_off + fc * 1024 + (half - d0 + 1) * 512]
                P.op("pe", lambda e, _o=ps, _l=lhsT, _r=rhs, _s=(fc == 0),
                     _e=(fc == NFC - 1): e.matmul(_o, _l, _r, start=_s,
                                                  stop=_e),
                     reads=[lhsT, rhs], writes=[ps], tag="oproj")
        blk.residual_hook(C, blk, qtg, pslist)


def simple_residual(C, blk, qtg, pslist):
    P = C.P
    xb = C.XBUF[:, 0:1024]
    P.dma("qsp", xb, blk.x_res[qtg * 128:(qtg + 1) * 128, :], tag="xres_l")
    d0, d1 = blk.out_dchunks
    for i, half in enumerate(range(d0, d1)):
        ps = pslist[i]
        dst = C.SCR[:, 1024 + half * 512:1024 + (half + 1) * 512]
        xs = xb[:, half * 512:(half + 1) * 512]
        P.op("dve", lambda e, _d=dst, _p=ps, _x=xs: e.tensor_add(_d, _p, _x),
             reads=[ps, xs], writes=[dst], tag="res_add")
    for half in range(0, 2):
        if d0 <= half < d1:
            continue
        dst = C.SCR[:, 1024 + half * 512:1024 + (half + 1) * 512]
        xs = xb[:, half * 512:(half + 1) * 512]
        P.op("dve", lambda e, _d=dst, _x=xs: e.tensor_copy(_d, _x),
             reads=[xs], writes=[dst], tag="res_copy")
    P.dma("qsp", blk.x_out[qtg * 128:(qtg + 1) * 128, :], C.SCR[:, 1024:2048],
          tag="xres_w")



class Blk:
    pass


def emit_L0_attn(C, x_in, x_out, W):
    nc = C.nc
    P = C.P
    blk = Blk()
    blk.variant = "std"
    blk.causal = True
    blk.logit_scale = 0.125
    blk.x_in = x_in
    blk.x_res = x_in
    blk.x_out = x_out
    blk.q_bias = None
    blk.k_bias = None
    blk.v_bias = None
    blk.q_evac_scale = 1.0
    blk.pre_ln_hook = None
    blk.qn_hook = None
    blk.kn_hook = None
    blk.out_dchunks = (0, 2)
    blk.wo_off = 0
    blk.ctx_evac = std_ctx_evac
    blk.residual_hook = simple_residual
    blk.qscr = nc.dram_tensor("l0a_qscr", [D, C.T], BF16, kind="Internal").ap()
    blk.xfscr = nc.dram_tensor("l0a_xfscr", [D, C.T], BF16, kind="Internal").ap()

    def headwise_pairs(wname):
        flat = W[wname].ap().rearrange("h d k -> h (d k)")
        return [(0, D, hh * HD, (hh + 1) * HD, flat[hh:hh + 1, :])
                for hh in range(H)]

    wq = WSpec(C, "l0a_q", None, D, D, "direct", headwise_pairs("L0_attn_P_q"))
    wk = WSpec(C, "l0a_k", None, D, D, "direct", headwise_pairs("L0_attn_P_k"))
    wv = WSpec(C, "l0a_v", None, D, D, "direct", headwise_pairs("L0_attn_P_v"))
    wo = WSpec(C, "l0a_o", None, D, D, "trans", headwise_pairs("L0_attn_P_o"))

    def load_w(C, which):
        ws = {"q": wq, "k": wk, "v": wv}[which]
        for dc in range(NFC):
            ws.load_slab(C.WBUF[:, dc * 1024:(dc + 1) * 1024],
                         dc * 128, (dc + 1) * 128, 0, D)

    def load_wo(C):
        for fc in range(NFC):
            wo.load_slab(C.WBUF[:, fc * 1024:(fc + 1) * 1024],
                         fc * 128, (fc + 1) * 128, 0, D)

    blk.load_w = load_w
    blk.load_wo = load_wo
    attn_phase_A(C, blk)
    attn_phase_B(C, blk)


def load_bias_cols(C, dram_ap_flat, dst_cols):
    src = dram_ap_flat.rearrange("(f p) -> p f", p=128)
    C.P.dma("qsp", dst_cols, src, tag="bias_cols", allow_slow_non_contiguous=True)


def bcast_vec_tile(C, dram_vec_tensor, dst_tile_f32, n=D):
    src = bass.AP(dram_vec_tensor, 0, [[0, 128], [1, n]])
    C.P.dma("qsp", dst_tile_f32, src, tag="bcast_vec")


def emit_L1_attn(C, x_in, x_out, W):
    nc = C.nc
    P = C.P
    blk = Blk()
    blk.variant = "std"
    blk.causal = False
    blk.logit_scale = 0.125
    blk.x_in = x_in
    blk.x_res = x_in
    blk.x_out = x_out
    blk.q_evac_scale = 1.0
    blk.pre_ln_hook = None
    blk.qn_hook = None
    blk.kn_hook = None
    blk.out_dchunks = (0, 2)
    blk.wo_off = 0
    blk.ctx_evac = std_ctx_evac
    blk.qscr = nc.dram_tensor("l1a_qscr", [D, C.T], BF16, kind="Internal").ap()
    blk.xfscr = nc.dram_tensor("l1a_xfscr", [D, C.T], BF16, kind="Internal").ap()

    wq = WSpec(C, "l1a_q", W["L1_attn_Wq"].ap().rearrange("d h k -> d (h k)"),
               D, D, "direct")
    wk = WSpec(C, "l1a_k", W["L1_attn_Wk"].ap().rearrange("d h k -> d (h k)"),
               D, D, "direct")
    wv = WSpec(C, "l1a_v", W["L1_attn_Wv"].ap().rearrange("d h k -> d (h k)"),
               D, D, "direct")
    wo = WSpec(C, "l1a_o", W["L1_attn_Wo"].ap().rearrange("h k d -> (h k) d"),
               D, D, "direct")
    blk.q_bias = C.STAT[:, 66:74]
    blk.k_bias = C.STAT[:, 74:82]
    blk.v_bias = C.STAT[:, 82:90]
    load_bias_cols(C, W["L1_attn_bq"].ap().rearrange("h k -> (h k)"), blk.q_bias)
    load_bias_cols(C, W["L1_attn_bk"].ap().rearrange("h k -> (h k)"), blk.k_bias)
    load_bias_cols(C, W["L1_attn_bv"].ap().rearrange("h k -> (h k)"), blk.v_bias)
    bo_t = C.VECROW[:, 0:1024]
    ga_t = C.VECROW[:, 1024:2048]
    beta_bc = bass.AP(W["L1_attn_beta"], 0, [[0, 128], [1, 1024]])
    bcast_vec_tile(C, W["L1_attn_bo"], bo_t)
    bcast_vec_tile(C, W["L1_attn_gamma"], ga_t)

    def load_w(C, which):
        ws = {"q": wq, "k": wk, "v": wv}[which]
        for dc in range(NFC):
            ws.load_slab(C.WBUF[:, dc * 1024:(dc + 1) * 1024],
                         dc * 128, (dc + 1) * 128, 0, D)

    def load_wo(C):
        for fc in range(NFC):
            wo.load_slab(C.WBUF[:, fc * 1024:(fc + 1) * 1024],
                         fc * 128, (fc + 1) * 128, 0, D)

    def residual(C, blk, qtg, pslist):
        xb = C.XBUF[:, 0:1024]
        P.dma("qsp", xb, blk.x_res[qtg * 128:(qtg + 1) * 128, :], tag="xres_l")
        ci = qtg % 32
        negmu = C.STAT[:, ci:ci + 1]
        rstd = C.STAT[:, 32 + ci:33 + ci]
        S1 = C.SCR[:, 1024:2048]
        P.op("dve", lambda e: e.tensor_scalar(S1, xb, negmu, rstd,
                                              ALU.add, ALU.mult),
             reads=[xb, negmu, rstd], writes=[S1], tag="l1_xn")
        P.dma("qpool", xb, beta_bc, tag="l1_beta_acc", accum_op=ALU.add)
        for i, ps in enumerate(pslist):
            sl = S1[:, i * 512:(i + 1) * 512]
            P.op("dve", lambda e, _s=sl, _p=ps: e.tensor_add(_s, _p, _s),
                 reads=[ps, sl], writes=[sl], tag="l1_addp")
            bl = bo_t[:, i * 512:(i + 1) * 512]
            P.op("dve", lambda e, _s=sl, _b=bl: e.tensor_add(_s, _s, _b),
                 reads=[sl, bl], writes=[sl], tag="l1_addbo")
        nm2 = C.STAT[:, 90:91]
        v2 = C.STAT[:, 91:92]
        r2 = C.STAT[:, 92:93]
        P.op("dve", lambda e: e.tensor_reduce(nm2, S1, AX.X, ALU.add),
             reads=[S1], writes=[nm2], tag="l1_sum")
        P.op("dve", lambda e: e.tensor_scalar_mul(nm2, nm2, -1.0 / D),
             reads=[nm2], writes=[nm2], tag="l1_nm")
        sq = C.SCR[:, 0:1024]
        P.op("act", lambda e: e.activation(sq, S1, AF.Square, bias=nm2,
                                           accum_out=v2),
             reads=[S1, nm2], writes=[sq, v2], tag="l1_sq")
        epsc = const_col(C, 1e-3)
        P.op("act", lambda e: e.activation(v2, v2, AF.Sqrt, scale=1.0 / D,
                                           bias=epsc),
             reads=[v2, epsc], writes=[v2], tag="l1_sd")
        P.op("dve", lambda e: e.reciprocal(r2, v2),
             reads=[v2], writes=[r2], tag="l1_rstd")
        P.op("dve", lambda e: e.tensor_scalar(S1, S1, nm2, r2, ALU.add,
                                              ALU.mult),
             reads=[S1, nm2, r2], writes=[S1], tag="l1_norm")
        P.op("dve", lambda e: e.tensor_mul(S1, S1, ga_t),
             reads=[S1, ga_t], writes=[S1], tag="l1_gamma")
        P.op("dve", lambda e: e.tensor_add(S1, S1, xb),
             reads=[S1, xb], writes=[S1], tag="l1_resx")
        P.dma("qsp", blk.x_out[qtg * 128:(qtg + 1) * 128, :], S1, tag="xres_w")

    blk.load_w = load_w
    blk.load_wo = load_wo
    blk.residual_hook = residual
    attn_phase_A(C, blk)
    attn_phase_B(C, blk)



def ffn_prep(C, blk):
    P = C.P
    T = C.T
    for tb in range(C.NTB):
        for tt in range(4):
            ti = tb * 4 + tt
            xn = C.SCRB[:, XN_OFF + tt * 1024:XN_OFF + (tt + 1) * 1024]
            negmu, rstd, xb = ln_tile(C, blk.x_in[ti * 128:(ti + 1) * 128, :],
                                      ti, tt % 2, xn)
            if blk.pre_ln_hook is not None:
                blk.pre_ln_hook(C, blk, ti, xb, negmu, rstd, xn)
            dsts = [C.BIGK[:, dc * T + ti * 128:dc * T + (ti + 1) * 128]
                    for dc in range(NFC)]
            transpose_to_fm(C, xn, tt, dsts)
            if blk.tile_hook is not None:
                blk.tile_hook(C, blk, ti)


def load_small_T(C, wspec, ncols, dst_cols, tag=""):
    P = C.P
    tmp = C.SCRB[0:ncols, STG_OFF:STG_OFF + 1024]
    P.dma("qact", tmp, wspec.cast[0:ncols, 0:D], tag=f"wsm_{tag}")
    for dc in range(NFC):
        pt = C.PAIR2[:, (dc % 2) * 512:(dc % 2) * 512 + (ncols + 1) // 2].bitcast(BF16)
        pt = bass.AP(pt.tensor, pt.offset, [[pt.ap[0][0], 128], [1, ncols]])
        src = tmp[0:ncols, dc * 128:(dc + 1) * 128]
        P.op("pe", lambda e, _p=pt, _s=src: e.transpose(
            _p, _s, C.IDB[0:ncols, 0:ncols]),
            reads=[src, C.IDB[:, :]], writes=[pt], tag="wsm_t")
        dst = dst_cols[:, dc * ncols:(dc + 1) * ncols]
        P.op("dve", lambda e, _d=dst, _p=pt: e.tensor_copy(_d, _p),
             reads=[pt], writes=[dst], tag="wsm_ev")


def small_logits(C, ti, wsm_cols, ncols, psum_slice, tag=""):
    P = C.P
    T = C.T
    for dc in range(NFC):
        lhsT = C.BIGK[:, dc * T + ti * 128:dc * T + (ti + 1) * 128]
        rhs = wsm_cols[:, dc * ncols:(dc + 1) * ncols]
        P.op("pe", lambda e, _o=psum_slice, _l=lhsT, _r=rhs, _s=(dc == 0),
             _e=(dc == NFC - 1): e.matmul(_o, _l, _r, start=_s, stop=_e),
             reads=[lhsT, rhs], writes=[psum_slice], tag=f"sl_{tag}")


def zeros8(C):
    c = const_col(C, 0.0)
    return bass.AP(c.tensor, c.offset, [[c.ap[0][0], 128], [0, 8]])


def emit_L2_ffn(C, x_in, x_out, W):
    nc = C.nc
    P = C.P
    T = C.T
    blk = Blk()
    blk.x_in = x_in
    blk.pre_ln_hook = None
    wg = WSpec(C, "l2f_g", W["L2_ffn_gate_w"].ap(), 8, D, "direct")
    we = WSpec(C, "l2f_e", W["L2_ffn_expert_w"].ap().rearrange("e f h -> (e f) h"),
               8 * D, D, "trans")
    wsm = C.SCRB[:, 9216:9280]
    load_small_T(C, wg, 8, wsm, "l2g")

    def tile_hook(C, blk, ti):
        ps = C.PAIR3[:, 512:520]
        small_logits(C, ti, wsm, 8, ps, "l2g")
        pr = C.SCR[:, 2304:2312]
        P.op("dve", lambda e: e.tensor_copy(pr, ps), reads=[ps], writes=[pr],
             tag="l2g_ev")
        m1 = C.SCR[:, 2312:2313]
        P.op("dve", lambda e: e.tensor_reduce(m1, pr, AX.X, ALU.max),
             reads=[pr], writes=[m1], tag="l2g_max")
        eq = C.SCR[:, 2320:2328]
        P.op("dve", lambda e: e.tensor_scalar(eq, pr, m1, None, ALU.is_equal),
             reads=[pr, m1], writes=[eq], tag="l2g_eq")
        cum = C.SCR[:, 2328:2336]
        z8 = zeros8(C)
        P.op("dve", lambda e: e.tensor_tensor_scan(cum, eq, z8, 0.0,
                                                   ALU.add, ALU.add),
             reads=[eq, C.CONSTF[:, :]], writes=[cum], tag="l2g_scan")
        P.op("dve", lambda e: e.tensor_scalar(cum, cum, 1.0, None, ALU.is_equal),
             reads=[cum], writes=[cum], tag="l2g_c1")
        fcol = C.SCR[:, 2048 + (ti % 32) * 8:2048 + (ti % 32) * 8 + 8]
        P.op("dve", lambda e: e.tensor_mul(fcol, eq, cum),
             reads=[eq, cum], writes=[fcol], tag="l2g_f")

    blk.tile_hook = tile_hook
    ffn_prep(C, blk)
    for e in range(8):
        for hc in range(NFC):
            dst = C.WBUF[:, hc * 1024:(hc + 1) * 1024]
            we.load_slab(dst, hc * 128, (hc + 1) * 128,
                         e * D, (e + 1) * D)
        for ti in range(C.NT):
            fcol = C.SCR[:, 2048 + (ti % 32) * 8 + e:2048 + (ti % 32) * 8 + e + 1]
            for half in range(2):
                ps = (C.PAIR0 if half == 0 else C.PAIR1)[:, 0:512]
                for dc in range(NFC):
                    lhsT = C.BIGK[:, dc * T + ti * 128:dc * T + (ti + 1) * 128]
                    rhs = C.WBUF[:, dc * 1024 + half * 512:dc * 1024 + (half + 1) * 512]
                    P.op("pe", lambda ee, _o=ps, _l=lhsT, _r=rhs, _s=(dc == 0),
                         _e=(dc == NFC - 1): ee.matmul(_o, _l, _r, start=_s,
                                                       stop=_e),
                         reads=[lhsT, rhs], writes=[ps], tag="l2f_mm")
                acc = C.BIGV[:, ti * 1024 + half * 512:ti * 1024 + (half + 1) * 512]
                if e == 0:
                    P.op("dve", lambda ee, _a=acc, _p=ps, _f=fcol:
                         ee.tensor_scalar(_a, _p, _f, None, ALU.mult),
                         reads=[ps, fcol], writes=[acc], tag="l2f_acc0")
                else:
                    P.op("dve", lambda ee, _a=acc, _p=ps, _f=fcol:
                         ee.scalar_tensor_tensor(_a, _p, _f, _a, ALU.mult,
                                                 ALU.add),
                         reads=[ps, fcol, acc], writes=[acc], tag="l2f_acc")
    for ti in range(C.NT):
        xb = C.XBUF[:, 0:1024]
        P.dma("qsp", xb, x_in[ti * 128:(ti + 1) * 128, :], tag="l2f_xl")
        ci = ti % 32
        negmu = C.STAT[:, ci:ci + 1]
        rstd = C.STAT[:, 32 + ci:33 + ci]
        S1 = C.SCR[:, 1024:2048]
        P.op("dve", lambda e, _x=xb, _n=negmu, _r=rstd: e.tensor_scalar(
            S1, _x, _n, _r, ALU.add, ALU.mult),
            reads=[xb, negmu, rstd], writes=[S1], tag="l2f_xn")
        acc = C.BIGV[:, ti * 1024:(ti + 1) * 1024]
        P.op("dve", lambda e, _a=acc: e.tensor_add(S1, S1, _a),
             reads=[S1, acc], writes=[S1], tag="l2f_add")
        P.op("dve", lambda e, _x=xb: e.tensor_add(S1, S1, _x),
             reads=[S1, xb], writes=[S1], tag="l2f_resx")
        P.dma("qsp", x_out[ti * 128:(ti + 1) * 128, :], S1, tag="l2f_xw")


L3F_OCS = [list(range(e * 8, e * 8 + 8)) for e in range(4)] + \
          [list(range(32 + (e - 4) * 16, 32 + (e - 3) * 16)) for e in range(4, 8)]


def emit_L3_ffn(C, x_in, x_out, W):
    nc = C.nc
    P = C.P
    T = C.T
    blk = Blk()
    blk.x_in = x_in
    blk.pre_ln_hook = None
    wr = WSpec(C, "l3f_r", W["L3_ffn_router_w"].ap(), 8, D, "direct")
    w1 = WSpec(C, "l3f_w1", W["L3_ffn_w1"].ap(), D, 12288, "direct")
    w2 = WSpec(C, "l3f_w2", W["L3_ffn_w2"].ap(), 12288, D, "direct")
    wsm = C.SCRB[:, 9280:9344]
    load_small_T(C, wr, 8, wsm, "l3r")

    def tile_hook(C, blk, ti):
        ci = ti % 32
        ps = C.PAIR3[:, 512:520]
        small_logits(C, ti, wsm, 8, ps, "l3r")
        pr = C.SCR[:, 2304:2312]
        P.op("act", lambda e: e.activation(pr, ps, AF.Sigmoid),
             reads=[ps], writes=[pr], tag="l3r_sig")
        m1 = C.SCR[:, 2312:2313]
        m2 = C.SCR[:, 2313:2314]
        sc = C.SCR[:, 2314:2315]
        P.op("dve", lambda e: e.tensor_reduce(m1, pr, AX.X, ALU.max),
             reads=[pr], writes=[m1], tag="l3r_m1")
        eq = C.SCR[:, 2320:2328]
        cum = C.SCR[:, 2328:2336]
        f1 = C.SCR[:, 2336:2344]
        pm = C.SCR[:, 2344:2352]
        z8 = zeros8(C)
        P.op("dve", lambda e: e.tensor_scalar(eq, pr, m1, None, ALU.is_equal),
             reads=[pr, m1], writes=[eq], tag="l3r_eq1")
        P.op("dve", lambda e: e.tensor_tensor_scan(cum, eq, z8, 0.0,
                                                   ALU.add, ALU.add),
             reads=[eq, C.CONSTF[:, :]], writes=[cum], tag="l3r_sc1")
        P.op("dve", lambda e: e.tensor_scalar(cum, cum, 1.0, None, ALU.is_equal),
             reads=[cum], writes=[cum], tag="l3r_c1")
        P.op("dve", lambda e: e.tensor_mul(f1, eq, cum),
             reads=[eq, cum], writes=[f1], tag="l3r_f1")
        P.op("dve", lambda e: e.scalar_tensor_tensor(pm, f1, -1e9, pr,
                                                     ALU.mult, ALU.add),
             reads=[f1, pr], writes=[pm], tag="l3r_pm")
        P.op("dve", lambda e: e.tensor_reduce(m2, pm, AX.X, ALU.max),
             reads=[pm], writes=[m2], tag="l3r_m2")
        P.op("dve", lambda e: e.tensor_scalar(eq, pm, m2, None, ALU.is_equal),
             reads=[pm, m2], writes=[eq], tag="l3r_eq2")
        P.op("dve", lambda e: e.tensor_tensor_scan(cum, eq, z8, 0.0,
                                                   ALU.add, ALU.add),
             reads=[eq, C.CONSTF[:, :]], writes=[cum], tag="l3r_sc2")
        P.op("dve", lambda e: e.tensor_scalar(cum, cum, 1.0, None, ALU.is_equal),
             reads=[cum], writes=[cum], tag="l3r_c2")
        P.op("dve", lambda e: e.tensor_mul(eq, eq, cum),
             reads=[eq, cum], writes=[eq], tag="l3r_f2")
        P.op("dve", lambda e: e.tensor_add(f1, f1, eq),
             reads=[f1, eq], writes=[f1], tag="l3r_f12")
        P.op("dve", lambda e: e.tensor_add(sc, m1, m2),
             reads=[m1, m2], writes=[sc], tag="l3r_s")
        P.op("dve", lambda e: e.tensor_scalar_add(sc, sc, 1e-20),
             reads=[sc], writes=[sc], tag="l3r_se")
        P.op("dve", lambda e: e.reciprocal(sc, sc),
             reads=[sc], writes=[sc], tag="l3r_r")
        P.op("dve", lambda e: e.tensor_mul(pm, pr, f1),
             reads=[pr, f1], writes=[pm], tag="l3r_sel")
        P.op("dve", lambda e: e.tensor_scalar(pm, pm, sc, None, ALU.mult),
             reads=[pm, sc], writes=[pm], tag="l3r_comb")
        cro = C.VECROW[:, ci * 8:ci * 8 + 8]
        P.op("act", lambda e: e.activation(cro, pm, AF.Sqrt),
             reads=[pm], writes=[cro], tag="l3r_root")

    blk.tile_hook = tile_hook
    ffn_prep(C, blk)

    for tb in range(C.NTB):
        for e in range(8):
            ocs = L3F_OCS[e]
            par = e % 2
            crow = C.VECROW[0:1, 256 + par * 512:768 + par * 512]
            for tt in range(4):
                ti = tb * 4 + tt
                col = C.VECROW[:, (ti % 32) * 8 + e:(ti % 32) * 8 + e + 1]
                col_to_row(C, col, C.VECROW[0:1, 256 + par * 512 + tt * 128:
                                            256 + par * 512 + (tt + 1) * 128])
            crb = C.VECROW[:, 1280 + par * 512:1792 + par * 512]
            bps = C.PAIR3[0:128, 0:512]
            bcast_row(C, crow, 128, bps)
            P.op("dve", lambda e2, _c=crb, _p=bps: e2.tensor_copy(_c, _p),
                 reads=[bps], writes=[crb], tag="l3f_crb")
            for i, oc in enumerate(ocs):
                wb = C.WBUF[:, (i % 2) * 1024:(i % 2) * 1024 + 1024]
                src = bass.AP(w1.cast.tensor, oc * 128,
                              [[12288, 128], [128 * 12288, NFC], [1, 128]])
                P.dma("qact", wb[:, 0:1024], src, tag="l3f_w1")
                ps = C.PAIR2[:, (i % 2) * 512:(i % 2) * 512 + 512]
                for dc in range(NFC):
                    lhsT = wb[:, dc * 128:(dc + 1) * 128]
                    rhs = C.BIGK[:, dc * T + tb * 512:dc * T + (tb + 1) * 512]
                    P.op("pe", lambda e2, _o=ps, _l=lhsT, _r=rhs, _s=(dc == 0),
                         _e=(dc == NFC - 1): e2.matmul(_o, _l, _r, start=_s,
                                                       stop=_e),
                         reads=[lhsT, rhs], writes=[ps], tag="l3f_h")
                tmp = C.MASKQ[:, (i % 2) * 512:(i % 2) * 512 + 512]
                P.op("dve", lambda e2, _t=tmp, _p=ps, _c=crb:
                     e2.scalar_tensor_tensor(_t, _p, 0.0, _c, ALU.max, ALU.mult),
                     reads=[ps, crb], writes=[tmp], tag="l3f_relu")
                g = C.BIGV[:, i * 512:(i + 1) * 512]
                P.op("act", lambda e2, _g=g, _t=tmp: e2.activation(
                    _g, _t, AF.Square),
                    reads=[tmp], writes=[g], tag="l3f_sq")
            for i, oc in enumerate(ocs):
                if i < 9:
                    dst = C.WBUF[:, 2048 + i * 1024:2048 + (i + 1) * 1024]
                else:
                    dst = C.SCRB[:, (i - 9) * 1024:(i - 8) * 1024]
                P.dma("qact", dst, w2.cast[oc * 128:(oc + 1) * 128, 0:D],
                      tag="l3f_w2")
            for qt in range(4):
                qtg = tb * 4 + qt
                pso = (C.PAIR0 if qt % 2 == 0 else C.PAIR1)
                for i, oc in enumerate(ocs):
                    lhsT = C.BIGV[:, i * 512 + qt * 128:i * 512 + (qt + 1) * 128]
                    if i < 9:
                        rhs_t = C.WBUF[:, 2048 + i * 1024:2048 + (i + 1) * 1024]
                    else:
                        rhs_t = C.SCRB[:, (i - 9) * 1024:(i - 8) * 1024]
                    for half in range(2):
                        ps = pso[:, half * 512:(half + 1) * 512]
                        rhs = rhs_t[:, half * 512:(half + 1) * 512]
                        P.op("pe", lambda e2, _o=ps, _l=lhsT, _r=rhs,
                             _s=(i == 0), _e=(i == len(ocs) - 1):
                             e2.matmul(_o, _l, _r, start=_s, stop=_e),
                             reads=[lhsT, rhs], writes=[ps], tag="l3f_y")
                for half in range(2):
                    ps = pso[:, half * 512:(half + 1) * 512]
                    ya = C.SCR[:, qt * 1024 + half * 512:qt * 1024 + (half + 1) * 512]
                    if e == 0:
                        P.op("dve", lambda e2, _y=ya, _p=ps: e2.tensor_copy(
                            _y, _p), reads=[ps], writes=[ya], tag="l3f_y0")
                    else:
                        P.op("dve", lambda e2, _y=ya, _p=ps: e2.tensor_add(
                            _y, _p, _y), reads=[ps, ya], writes=[ya],
                            tag="l3f_yacc")
        for qt in range(4):
            qtg = tb * 4 + qt
            xb = C.XBUF[:, 0:1024]
            P.dma("qsp", xb, x_in[qtg * 128:(qtg + 1) * 128, :], tag="l3f_xl")
            ya = C.SCR[:, qt * 1024:(qt + 1) * 1024]
            P.op("dve", lambda e2, _y=ya, _x=xb: e2.tensor_add(_y, _y, _x),
                 reads=[ya, xb], writes=[ya], tag="l3f_res")
            P.dma("qsp", x_out[qtg * 128:(qtg + 1) * 128, :], ya, tag="l3f_xw")


L0F_EPS = 1.1920929e-07


def emit_L0_ffn(C, x_in, x_out, W):
    nc = C.nc
    P = C.P
    T = C.T
    blk = Blk()
    blk.variant = "std"
    blk.causal = True
    blk.logit_scale = 0.125
    blk.x_in = x_in
    blk.q_bias = None
    blk.k_bias = None
    blk.v_bias = None
    blk.q_evac_scale = 1.0
    blk.qn_hook = None
    blk.kn_hook = None
    blk.out_dchunks = (0, 2)
    blk.wo_off = 0
    blk.ctx_evac = std_ctx_evac
    blk.qscr = nc.dram_tensor("l0f_qscr", [D, C.T], BF16, kind="Internal").ap()
    blk.xfscr = nc.dram_tensor("l0f_xfscr", [D, C.T], BF16, kind="Internal").ap()
    uscr = nc.dram_tensor("l0f_u", [T, D], F32, kind="Internal").ap()

    wq = WSpec(C, "l0f_q", W["L0_ffn_wq"].ap(), D, D, "trans")
    wk = WSpec(C, "l0f_k", W["L0_ffn_wk"].ap(), D, D, "trans")
    wv = WSpec(C, "l0f_v", W["L0_ffn_wv"].ap(), D, D, "trans")
    wo = WSpec(C, "l0f_o", W["L0_ffn_wo"].ap(), D, D, "trans")
    n1c = C.STAT[:, 66:74]
    n2c = C.STAT[:, 74:82]
    load_bias_cols(C, W["L0_ffn_n1_w"].ap(), n1c)
    load_bias_cols(C, W["L0_ffn_n2_w"].ap(), n2c)
    g1 = C.SCR[0:1, 512:576]
    g2 = C.SCR[0:1, 576:640]
    P.dma("qsp", g1, W["L0_ffn_qn_w"].ap()[0:64], tag="l0f_qn")
    P.dma("qsp", g2, W["L0_ffn_kn_w"].ap()[0:64], tag="l0f_kn")
    P.op("dve", lambda e: e.tensor_mul(g1, g1, g2), reads=[g1, g2],
         writes=[g1], tag="l0f_g")
    gps = C.PS6[0:128, 0:512]
    P.op("pe", lambda e: e.matmul(gps[:, 0:64], ones_row(C, 128), g1,
                                  start=True, stop=True),
         reads=[ones_row(C, 128), g1], writes=[gps], tag="l0f_gb")
    gt = C.VECROW[:, 0:64]
    P.op("dve", lambda e: e.tensor_copy(gt, gps[:, 0:64]),
         reads=[gps], writes=[gt], tag="l0f_gt")

    def load_w(C, which):
        ws = {"q": wq, "k": wk, "v": wv}[which]
        for dc in range(NFC):
            ws.load_slab(C.WBUF[:, dc * 1024:(dc + 1) * 1024],
                         dc * 128, (dc + 1) * 128, 0, D)

    def w_post(C, which):
        for dc in range(NFC):
            sl = C.WBUF[:, dc * 1024:(dc + 1) * 1024]
            col = n1c[:, dc:dc + 1]
            P.op("dve", lambda e, _s=sl, _c=col: e.tensor_scalar_mul(
                _s, _s, _c), reads=[sl, col], writes=[sl], tag="l0f_n1w")

    def pre_hook(C, blk, ti, xb, negmu, rstd, xn):
        sq = C.SCR[:, 0:1024]
        tmp = C.STAT[:, 95:96]
        P.op("act", lambda e: e.activation(sq, xn, AF.Square, accum_out=tmp),
             reads=[xn], writes=[sq, tmp], tag="l0f_zsq")
        epsc = const_col(C, L0F_EPS)
        P.op("act", lambda e: e.activation(tmp, tmp, AF.Sqrt, scale=1.0 / D,
                                           bias=epsc),
             reads=[tmp, epsc], writes=[tmp], tag="l0f_zsd")
        ra = C.STAT[:, 96 + ti % 32:97 + ti % 32]
        P.op("dve", lambda e: e.reciprocal(ra, tmp),
             reads=[tmp], writes=[ra], tag="l0f_ra")
        P.op("dve", lambda e: e.tensor_scalar_mul(xn, xn, ra),
             reads=[xn, ra], writes=[xn], tag="l0f_h1")

    def qk_custom(C, blk, which, tb):
        for tt in range(4):
            ti = tb * 4 + tt
            qtm = C.SCR[:, 0:1024]
            for half in range(2):
                ps = (C.PAIR0 if half == 0 else C.PAIR1)[:, 0:512]
                for dc in range(NFC):
                    lhsT = C.SCRB[:, XNFM_OFF + dc * 512 + tt * 128:
                                  XNFM_OFF + dc * 512 + (tt + 1) * 128]
                    rhs = C.WBUF[:, dc * 1024 + half * 512:
                                  dc * 1024 + (half + 1) * 512]
                    P.op("pe", lambda e, _o=ps, _l=lhsT, _r=rhs, _s=(dc == 0),
                         _e=(dc == NFC - 1): e.matmul(_o, _l, _r, start=_s,
                                                      stop=_e),
                         reads=[lhsT, rhs], writes=[ps], tag="l0f_qk")
                sl = qtm[:, half * 512:(half + 1) * 512]
                P.op("dve", lambda e, _s=sl, _p=ps: e.tensor_copy(_s, _p),
                     reads=[ps], writes=[sl], tag="l0f_qkev")
            sq = C.SCR[:, 1024:2048]
            P.op("act", lambda e: e.activation(sq, qtm, AF.Square),
                 reads=[qtm], writes=[sq], tag="l0f_qsq")
            sums = C.SCR[:, 2304:2320]
            for h in range(H):
                sc = sums[:, h:h + 1]
                ssl = sq[:, h * 64:(h + 1) * 64]
                P.op("dve", lambda e, _c=sc, _s=ssl: e.tensor_reduce(
                    _c, _s, AX.X, ALU.add), reads=[ssl], writes=[sc],
                    tag="l0f_qsum")
            epsc = const_col(C, L0F_EPS)
            P.op("act", lambda e: e.activation(sums, sums, AF.Sqrt,
                                               scale=1.0 / 64, bias=epsc),
                 reads=[sums, epsc], writes=[sums], tag="l0f_qsd")
            rr = C.SCR[:, 2320:2336]
            P.op("dve", lambda e: e.reciprocal(rr, sums),
                 reads=[sums], writes=[rr], tag="l0f_qr")
            stage = C.SCRB[:, STG_OFF:STG_OFF + 1024]
            for h in range(H):
                qsl = qtm[:, h * 64:(h + 1) * 64]
                osl = stage[:, h * 64:(h + 1) * 64]
                rc = rr[:, h:h + 1]
                if which == "q":
                    t = C.SCR[:, 1024 + h * 64:1024 + (h + 1) * 64]
                    P.op("dve", lambda e, _t=t, _q=qsl, _r=rc:
                         e.tensor_scalar_mul(_t, _q, _r),
                         reads=[qsl, rc], writes=[t], tag="l0f_qrm")
                    P.op("dve", lambda e, _o=osl, _t=t: e.tensor_mul(
                        _o, _t, gt), reads=[t, gt], writes=[osl],
                        tag="l0f_qg")
                else:
                    P.op("dve", lambda e, _o=osl, _q=qsl, _r=rc:
                         e.tensor_scalar_mul(_o, _q, _r),
                         reads=[qsl, rc], writes=[osl], tag="l0f_krm")
            if which == "k":
                dsts = [C.BIGK[:, dc * T + ti * 128:dc * T + (ti + 1) * 128]
                        for dc in range(NFC)]
                transpose_to_fm(C, stage, tt, dsts)
            else:
                st2 = C.SCRB[:, 9344:9472]
                dsts = [st2] * NFC
                for dc in range(NFC):
                    pt = C.PAIR2[:, (dc % 2) * 512:(dc % 2) * 512 + 64].bitcast(BF16)
                    s = stage[:, dc * 128:(dc + 1) * 128]
                    P.op("pe", lambda e, _p=pt, _s=s: e.transpose(
                        _p, _s, C.IDB[:, :]),
                        reads=[s, C.IDB[:, :]], writes=[pt], tag="l0f_qt")
                    P.op("dve", lambda e, _d=st2, _p=pt: e.tensor_copy(_d, _p),
                         reads=[pt], writes=[st2], tag="l0f_qtev")
                    P.dma("qsp", blk.qscr[dc * 128:(dc + 1) * 128,
                                          ti * 128:(ti + 1) * 128], st2,
                          tag="l0f_qw")

    def residual_u(C, blk, qtg, pslist):
        xb = C.XBUF[:, 0:1024]
        P.dma("qsp", xb, x_in[qtg * 128:(qtg + 1) * 128, :], tag="l0f_xl")
        ci = qtg % 32
        negmu = C.STAT[:, ci:ci + 1]
        rstd = C.STAT[:, 32 + ci:33 + ci]
        S1 = C.SCR[:, 1024:2048]
        P.op("dve", lambda e: e.tensor_scalar(S1, xb, negmu, rstd,
                                              ALU.add, ALU.mult),
             reads=[xb, negmu, rstd], writes=[S1], tag="l0f_z")
        sl = S1[:, 256:512]
        P.op("dve", lambda e, _s=sl, _p=pslist[0][:, 256:512]: e.tensor_add(
            _s, _p, _s), reads=[pslist[0], sl], writes=[sl], tag="l0f_u0")
        sl1 = S1[:, 512:1024]
        P.op("dve", lambda e, _s=sl1, _p=pslist[1]: e.tensor_add(
            _s, _p, _s), reads=[pslist[1], sl1], writes=[sl1], tag="l0f_u1")
        P.dma("qsp", uscr[qtg * 128:(qtg + 1) * 128, :], S1, tag="l0f_uw")

    blk.pre_ln_hook = pre_hook
    blk.qk_custom = qk_custom
    blk.w_post_hook = w_post
    blk.load_w = load_w

    def load_wo(C):
        for fc in range(NFC):
            wo.load_slab(C.WBUF[:, fc * 1024:(fc + 1) * 1024],
                         fc * 128, (fc + 1) * 128, 0, D)

    blk.load_wo = load_wo
    blk.residual_hook = residual_u
    attn_phase_A(C, blk)
    attn_phase_B(C, blk)

    wgate = WSpec(C, "l0f_gate", W["L0_ffn_gate_w"].ap(), 4, D, "direct")
    exp_specs = {}
    for nm, rows, cols in (("e0_up", 2048, D), ("e0_down", D, 2048),
                           ("e1_w1", 2048, D), ("e1_w2", 2048, D),
                           ("e1_down", D, 2048), ("e2_l1", 2048, D),
                           ("e2_l2", 2048, 2048), ("e2_l3", 2048, 2048),
                           ("e2_l4", D, 2048), ("e3_down", D, D),
                           ("e3_up", 2048, D), ("e3_out", D, 2048)):
        exp_specs[nm] = WSpec(C, "l0f_" + nm, W["L0_ffn_" + nm].ap(),
                              rows, cols, "trans")

    fblk = Blk()
    fblk.x_in = uscr

    def moe_pre(C, fblk, ti, xb, negmu, rstd, xn):
        sq = C.SCR[:, 0:1024]
        tmp = C.STAT[:, 95:96]
        P.op("act", lambda e: e.activation(sq, xb, AF.Square, accum_out=tmp),
             reads=[xb], writes=[sq, tmp], tag="l0f_usq")
        epsc = const_col(C, L0F_EPS)
        P.op("act", lambda e: e.activation(tmp, tmp, AF.Sqrt, scale=1.0 / D,
                                           bias=epsc),
             reads=[tmp, epsc], writes=[tmp], tag="l0f_usd")
        rb = C.STAT[:, 96 + ti % 32:97 + ti % 32]
        P.op("dve", lambda e: e.reciprocal(rb, tmp),
             reads=[tmp], writes=[rb], tag="l0f_urs")
        P.op("dve", lambda e: e.tensor_scalar_mul(xn, xb, rb),
             reads=[xb, rb], writes=[xn], tag="l0f_xf")

    fblk.pre_ln_hook = moe_pre
    wsm = C.SCRB[:, 9216:9248]
    load_small_T(C, wgate, 4, wsm, "l0fg")
    for dc in range(NFC):
        sl = wsm[:, dc * 4:(dc + 1) * 4]
        col = n2c[:, dc:dc + 1]
        P.op("dve", lambda e, _s=sl, _c=col: e.tensor_scalar_mul(_s, _s, _c),
             reads=[sl, col], writes=[sl], tag="l0f_gn2")

    def gate_hook(C, fblk, ti):
        ps = C.PAIR3[:, 512:516]
        small_logits(C, ti, wsm, 4, ps, "l0fg")
        nm = C.SCR[:, 2312:2313]
        P.op("dve", lambda e: e.tensor_reduce(nm, ps, AX.X, ALU.max,
                                              negate=True),
             reads=[ps], writes=[nm], tag="l0fg_m")
        pr = C.SCR[:, 2304:2308]
        ssum = C.SCR[:, 2313:2314]
        P.op("act", lambda e: e.activation(pr, ps, AF.Exp, bias=nm,
                                           accum_out=ssum),
             reads=[ps, nm], writes=[pr, ssum], tag="l0fg_e")
        P.op("dve", lambda e: e.reciprocal(ssum, ssum),
             reads=[ssum], writes=[ssum], tag="l0fg_rs")
        P.op("dve", lambda e: e.tensor_scalar_mul(pr, pr, ssum),
             reads=[pr, ssum], writes=[pr], tag="l0fg_p")
        m1 = C.SCR[:, 2314:2315]
        m2 = C.SCR[:, 2315:2316]
        P.op("dve", lambda e: e.tensor_reduce(m1, pr, AX.X, ALU.max),
             reads=[pr], writes=[m1], tag="l0fg_m1")
        eq = C.SCR[:, 2320:2324]
        cum = C.SCR[:, 2324:2328]
        z8 = zeros8(C)
        z4 = bass.AP(z8.tensor, z8.offset, [[z8.ap[0][0], 128], [0, 4]])
        P.op("dve", lambda e: e.tensor_scalar(eq, pr, m1, None, ALU.is_equal),
             reads=[pr, m1], writes=[eq], tag="l0fg_eq")
        P.op("dve", lambda e: e.tensor_tensor_scan(cum, eq, z4, 0.0,
                                                   ALU.add, ALU.add),
             reads=[eq, C.CONSTF[:, :]], writes=[cum], tag="l0fg_sc")
        P.op("dve", lambda e: e.tensor_scalar(cum, cum, 1.0, None,
                                              ALU.is_equal),
             reads=[cum], writes=[cum], tag="l0fg_c1")
        P.op("dve", lambda e: e.tensor_mul(eq, eq, cum),
             reads=[eq, cum], writes=[eq], tag="l0fg_f1")
        pm = C.SCR[:, 2328:2332]
        P.op("dve", lambda e: e.scalar_tensor_tensor(pm, eq, -1e9, pr,
                                                     ALU.mult, ALU.add),
             reads=[eq, pr], writes=[pm], tag="l0fg_pm")
        P.op("dve", lambda e: e.tensor_reduce(m2, pm, AX.X, ALU.max),
             reads=[pm], writes=[m2], tag="l0fg_m2")
        f12 = C.SCR[:, 2332:2336]
        P.op("dve", lambda e: e.tensor_scalar(f12, pr, m2, None, ALU.is_ge),
             reads=[pr, m2], writes=[f12], tag="l0fg_f12")
        sc = C.SCR[:, 2316:2317]
        P.op("dve", lambda e: e.tensor_add(sc, m1, m2),
             reads=[m1, m2], writes=[sc], tag="l0fg_s")
        P.op("dve", lambda e: e.tensor_scalar_add(sc, sc, 1e-8),
             reads=[sc], writes=[sc], tag="l0fg_se")
        P.op("dve", lambda e: e.reciprocal(sc, sc),
             reads=[sc], writes=[sc], tag="l0fg_r")
        wcol = C.VECROW[:, 2304 + (ti % 32) * 4:2304 + (ti % 32) * 4 + 4]
        P.op("dve", lambda e: e.tensor_mul(wcol, pr, f12),
             reads=[pr, f12], writes=[wcol], tag="l0fg_w1")
        P.op("dve", lambda e: e.tensor_scalar_mul(wcol, wcol, sc),
             reads=[wcol, sc], writes=[wcol], tag="l0fg_w2")

    fblk.tile_hook = gate_hook
    ffn_prep(C, fblk)

    def fm_level(C, wspec, Kdim, Odim, src, dst_slot, actf, tb, n2fold,
                 wts_mul=None):
        st, sbase, sstride = src
        srs = st.ap().ap[0][0]
        ncc = Kdim // 128
        noc = Odim // 128
        ogrp = max(1, min(noc, (11264 // 128) // ncc))
        for og0 in range(0, noc, ogrp):
            og1 = min(og0 + ogrp, noc)
            ogw = (og1 - og0) * 128
            for cc in range(ncc):
                dst = C.WBUF[:, cc * ogrp * 128:cc * ogrp * 128 + ogw]
                wspec.load_slab(dst, cc * 128, (cc + 1) * 128,
                                og0 * 128, og0 * 128 + ogw)
                if n2fold:
                    col = n2c[:, cc:cc + 1]
                    P.op("dve", lambda e, _s=dst, _c=col:
                         e.tensor_scalar_mul(_s, _s, _c),
                         reads=[dst, col], writes=[dst], tag="l0f_wn2")
            for oc in range(og0, og1):
                ps = C.PAIR2[:, (oc % 2) * 512:(oc % 2) * 512 + 512]
                for cc in range(ncc):
                    lhsT = C.WBUF[:, cc * ogrp * 128 + (oc - og0) * 128:
                                  cc * ogrp * 128 + (oc - og0 + 1) * 128]
                    rhs = bass.AP(st, sbase + cc * sstride,
                                  [[srs, 128], [1, 512]])
                    P.op("pe", lambda e, _o=ps, _l=lhsT, _r=rhs, _s=(cc == 0),
                         _e=(cc == ncc - 1): e.matmul(_o, _l, _r, start=_s,
                                                      stop=_e),
                         reads=[rhs, lhsT], writes=[ps], tag="l0f_lv")
                g = C.BIGV[:, dst_slot + oc * 512:dst_slot + (oc + 1) * 512]
                if wts_mul is None:
                    P.op("act", lambda e, _g=g, _p=ps, _f=actf: e.activation(
                        _g, _p, _f), reads=[ps], writes=[g], tag="l0f_lvev")
                else:
                    tmpf = C.MASKQ[:, (oc % 2) * 512:(oc % 2) * 512 + 512]
                    P.op("act", lambda e, _t=tmpf, _p=ps, _f=actf:
                         e.activation(_t, _p, _f), reads=[ps], writes=[tmpf],
                         tag="l0f_lva")
                    P.op("dve", lambda e, _g=g, _t=tmpf, _w=wts_mul:
                         e.tensor_mul(_g, _t, _w), reads=[tmpf, wts_mul],
                         writes=[g], tag="l0f_lvw")

    for tb in range(C.NTB):
        ya = C.SCR[:, 0:4096]
        for e_i in range(4):
            par = e_i % 2
            crow = C.VECROW[0:1, 256 + par * 512:768 + par * 512]
            for tt in range(4):
                ti = tb * 4 + tt
                col = C.VECROW[:, 2304 + (ti % 32) * 4 + e_i:
                               2304 + (ti % 32) * 4 + e_i + 1]
                col_to_row(C, col, C.VECROW[0:1, 256 + par * 512 + tt * 128:
                                            256 + par * 512 + (tt + 1) * 128])
            wtb = C.VECROW[:, 1280 + par * 512:1792 + par * 512]
            bps = C.PAIR3[0:128, 0:512]
            bcast_row(C, crow, 128, bps)
            P.op("dve", lambda e2, _c=wtb, _p=bps: e2.tensor_copy(_c, _p),
                 reads=[bps], writes=[wtb], tag="l0f_wtb")
            XF = (C.BIGK, tb * 512, T)
            A = (C.BIGV, 0, 512)
            B = (C.BIGV, 8192, 512)
            Cs = (C.BIGV, 16384, 512)
            if e_i == 0:
                fm_level(C, exp_specs["e0_up"], D, 2048, XF, 0, AF.Gelu, tb,
                         True, wtb)
                down, hid = exp_specs["e0_down"], 2048
                glast = A
            elif e_i == 1:
                fm_level(C, exp_specs["e1_w1"], D, 2048, XF, 0, AF.Silu, tb,
                         True)
                fm_level(C, exp_specs["e1_w2"], D, 2048, XF, 16384, AF.Copy,
                         tb, True)
                for oc in range(16):
                    a = C.BIGV[:, oc * 512:(oc + 1) * 512]
                    c2 = C.BIGV[:, 16384 + oc * 512:16384 + (oc + 1) * 512]
                    b = C.BIGV[:, 8192 + oc * 512:8192 + (oc + 1) * 512]
                    tmpf = C.MASKQ[:, (oc % 2) * 512:(oc % 2) * 512 + 512]
                    P.op("dve", lambda e2, _t=tmpf, _a=a, _c=c2:
                         e2.tensor_mul(_t, _a, _c), reads=[a, c2],
                         writes=[tmpf], tag="l0f_e1p")
                    P.op("dve", lambda e2, _b=b, _t=tmpf, _w=wtb:
                         e2.tensor_mul(_b, _t, _w), reads=[tmpf, wtb],
                         writes=[b], tag="l0f_e1w")
                down, hid = exp_specs["e1_down"], 2048
                glast = B
            elif e_i == 2:
                fm_level(C, exp_specs["e2_l1"], D, 2048, XF, 0, AF.Gelu, tb,
                         True)
                fm_level(C, exp_specs["e2_l2"], 2048, 2048, A, 16384, AF.Gelu,
                         tb, False)
                fm_level(C, exp_specs["e2_l3"], 2048, 2048, Cs, 8192, AF.Gelu,
                         tb, False, wtb)
                down, hid = exp_specs["e2_l4"], 2048
                glast = B
            else:
                fm_level(C, exp_specs["e3_down"], D, D, XF, 0, AF.Gelu, tb,
                         True)
                fm_level(C, exp_specs["e3_up"], D, 2048, A, 8192, AF.Gelu,
                         tb, False, wtb)
                down, hid = exp_specs["e3_out"], 2048
                glast = B
            nhc = hid // 128
            for qt in range(4):
                pso = C.PAIR0 if qt % 2 == 0 else C.PAIR1
                for hc in range(nhc):
                    wb = C.WBUF[:, 8192 + (hc % 2) * 1024:
                                8192 + (hc % 2) * 1024 + 1024]
                    down.load_slab(wb, hc * 128, (hc + 1) * 128, 0, D)
                    lhsT = C.BIGV[:, glast[1] + hc * 512 + qt * 128:
                                  glast[1] + hc * 512 + (qt + 1) * 128]
                    for half in range(2):
                        ps = pso[:, half * 512:(half + 1) * 512]
                        rhs = wb[:, half * 512:(half + 1) * 512]
                        P.op("pe", lambda e2, _o=ps, _l=lhsT, _r=rhs,
                             _s=(hc == 0), _e=(hc == nhc - 1):
                             e2.matmul(_o, _l, _r, start=_s, stop=_e),
                             reads=[lhsT, rhs], writes=[ps], tag="l0f_dn")
                for half in range(2):
                    ps = pso[:, half * 512:(half + 1) * 512]
                    ysl = ya[:, qt * 1024 + half * 512:
                             qt * 1024 + (half + 1) * 512]
                    if e_i == 0:
                        P.op("dve", lambda e2, _y=ysl, _p=ps: e2.tensor_copy(
                            _y, _p), reads=[ps], writes=[ysl], tag="l0f_y0")
                    else:
                        P.op("dve", lambda e2, _y=ysl, _p=ps: e2.tensor_add(
                            _y, _p, _y), reads=[ps, ysl], writes=[ysl],
                            tag="l0f_ya")
        for qt in range(4):
            qtg = tb * 4 + qt
            xb = C.XBUF[:, 0:1024]
            P.dma("qpool", xb, x_in[qtg * 128:(qtg + 1) * 128, :], tag="l0f_x2")
            P.dma("qpool", xb, uscr[qtg * 128:(qtg + 1) * 128, :],
                  tag="l0f_u2", accum_op=ALU.add)
            S1 = C.MASKQ[:, 0:1024]
            P.op("dve", lambda e2, _x=xb: e2.tensor_copy(S1, _x),
                 reads=[xb], writes=[S1], tag="l0f_xu")
            ysl = ya[:, qt * 1024 + 256:qt * 1024 + 1024]
            s2 = S1[:, 256:1024]
            P.op("dve", lambda e2, _s=s2, _y=ysl: e2.tensor_add(_s, _s, _y),
                 reads=[s2, ysl], writes=[s2], tag="l0f_moe")
            P.dma("qsp", x_out[qtg * 128:(qtg + 1) * 128, :], S1, tag="l0f_xw2")


def emit_L2_attn(C, x_in, x_out, W):
    nc = C.nc
    P = C.P
    T = C.T
    blk = Blk()
    blk.variant = "l2"
    blk.causal = True
    blk.logit_scale = 1.0
    blk.x_in = x_in
    blk.x_res = x_in
    blk.x_out = x_out
    blk.q_bias = None
    blk.k_bias = None
    blk.v_bias = None
    blk.q_evac_scale = 0.125
    blk.pre_ln_hook = None
    blk.qn_hook = None
    blk.kn_hook = None
    blk.out_dchunks = (0, 2)
    blk.wo_off = 0
    blk.residual_hook = simple_residual
    blk.qscr = nc.dram_tensor("l2a_qscr", [D, C.T], BF16, kind="Internal").ap()
    blk.xfscr = nc.dram_tensor("l2a_xfscr", [D, C.T], BF16, kind="Internal").ap()
    wqkv = WSpec(C, "l2a_qkv", W["L2_attn_w_qkv"].ap(), 3 * D, D, "trans")
    wo = WSpec(C, "l2a_o", W["L2_attn_w_proj"].ap(), D, D, "trans")

    fillreg = nc.gpsimd.alloc_register("l2fill")
    REG_FILL = ("REG:l2fill", 0, 1, 0, 1)
    P.op("pool", lambda e: e.reg_mov(fillreg, -100.0), writes=[REG_FILL],
         tag="l2_freg")
    mm = C.VECROW[:, 0:1024]
    blk.maskm = mm
    P.op("pool", lambda e: e.memset(mm, 3.0e38), writes=[mm], tag="l2_mm0")
    P.op("pool", lambda e: e.affine_select(
        mm, mm, [[-1, 1024]], ALU.is_ge, fillreg, base=512,
        channel_multiplier=1),
        reads=[mm, REG_FILL], writes=[mm], tag="l2_mm")
    onesb = C.SCRB[:, 9472:9473]
    P.op("pool", lambda e: e.memset(onesb, 1.0), writes=[onesb], tag="l2_1b")

    def load_w(C, which):
        o0 = {"q": 0, "k": D, "v": 2 * D}[which]
        for dc in range(NFC):
            wqkv.load_slab(C.WBUF[:, dc * 1024:(dc + 1) * 1024],
                           dc * 128, (dc + 1) * 128, o0, o0 + D)

    def load_wo(C):
        for fc in range(NFC):
            wo.load_slab(C.WBUF[:, fc * 1024:(fc + 1) * 1024],
                         fc * 128, (fc + 1) * 128, 0, D)

    def l2_exp(C, blk, qb, h, jt, ps, pt):
        negm = C.SCR[0:1, 3072 + (h % 2) * 512:3584 + (h % 2) * 512]
        negb = C.VECROW[:, 1024:1536]
        if jt == 0:
            bps = C.PAIR3[0:128, 0:512]
            bcast_row(C, negm, 128, bps)
            P.op("dve", lambda e: e.tensor_copy(negb, bps),
                 reads=[bps], writes=[negb], tag="l2_negb")
        cl = C.MASKQ[:, (jt % 2) * 512:(jt % 2) * 512 + 512]
        P.op("dve", lambda e, _c=cl, _p=ps: e.tensor_scalar(
            _c, _p, -100.0, 100.0, ALU.max, ALU.min),
            reads=[ps], writes=[cl], tag="l2_clip")
        if jt >= qb * 4:
            d = qb * 512 - jt * 128
            P.op("pool", lambda e, _c=cl, _d=d: e.affine_select(
                _c, _c, [[1, 512]], ALU.is_ge, fillreg, base=_d,
                channel_multiplier=-1),
                reads=[cl, REG_FILL], writes=[cl], tag="l2_dmask")
        P.op("dve", lambda e, _c=cl: e.tensor_add(_c, _c, negb),
             reads=[cl, negb], writes=[cl], tag="l2_subm")
        P.op("act", lambda e, _t=pt, _c=cl: e.activation(_t, _c, AF.Exp),
             reads=[cl], writes=[pt], tag="l2_exp")

    blk.l2_exp = l2_exp

    def ctx_evac(C, blk, qb, h, ctx):
        negm = C.SCR[0:1, 3072 + (h % 2) * 512:3584 + (h % 2) * 512]
        njt = (qb + 1) * 4
        nskip = C.NT - njt
        rs = C.BIGV.ap().ap[0][0]
        vbase = 2176 if h % 2 == 0 else 2696
        vtrow = C.SCR[32:33, vbase + (h // 2) * 65:vbase + (h // 2) * 65 + 65]
        if qb == 0:
            vt_ps = C.PAIR2[0:65, 512:1024]
            for jt in range(C.NT):
                lhsV = bass.AP(C.BIGV, jt * VSTRIDE + h * 65,
                               [[rs, 128], [1, 65]])
                P.op("pe", lambda e, _v=lhsV, _p=vt_ps, _s=(jt == 0),
                     _e=(jt == C.NT - 1): e.matmul(_p[:, 0:1], _v, onesb,
                                                   start=_s, stop=_e),
                     reads=[lhsV, onesb], writes=[vt_ps], tag="l2_vt")
            vtc = C.SCR[0:65, 2753:2754]
            P.op("dve", lambda e: e.tensor_copy(vtc, vt_ps[:, 0:1]),
                 reads=[vt_ps], writes=[vtc], tag="l2_vtc")
            col_to_row(C, vtc, vtrow)
        if nskip > 0:
            leak = C.SCR[0:1, 1601:2113]
            n100 = const_col(C, -100.0)
            P.op("act", lambda e: e.activation(leak, negm, AF.Exp,
                                               bias=n100[0:1, 0:1]),
                 reads=[negm, n100], writes=[leak], tag="l2_leak")
            sf_ps = C.PAIR2[0:65, 512:1024]
            for i, jt in enumerate(range(njt, C.NT)):
                lhsV = bass.AP(C.BIGV, jt * VSTRIDE + h * 65,
                               [[rs, 128], [1, 65]])
                P.op("pe", lambda e, _v=lhsV, _p=sf_ps, _s=(i == 0),
                     _e=(i == nskip - 1): e.matmul(_p[:, 0:1], _v, onesb,
                                                   start=_s, stop=_e),
                     reads=[lhsV, onesb], writes=[sf_ps], tag="l2_sf")
            sfc = C.SCR[0:65, 2754:2755]
            P.op("dve", lambda e: e.tensor_copy(sfc, sf_ps[:, 0:1]),
                 reads=[sf_ps], writes=[sfc], tag="l2_sfc")
            sfrow = C.SCR[0:1, 2688:2753]
            col_to_row(C, sfc, sfrow)
            P.op("pe", lambda e: e.matmul(ctx[0:65, 0:512], sfrow, leak,
                                          start=False, stop=True),
                 reads=[sfrow, leak], writes=[ctx], tag="l2_leakmm")
        s1 = C.SCR[32:33, 512:1024]
        P.op("dve", lambda e: e.tensor_copy(s1, ctx[64:65, 0:512]),
             reads=[ctx[64:65, 0:512]], writes=[s1], tag="l2_s1")
        t1 = C.SCR[32:33, 1024:1536]
        t2 = C.SCR[32:33, 1536:2048]
        P.op("dve", lambda e: e.tensor_scalar_add(t1, s1, 1e-6),
             reads=[s1], writes=[t1], tag="l2_t")
        P.op("dve", lambda e: e.reciprocal(t1, t1),
             reads=[t1], writes=[t1], tag="l2_r1")
        P.op("dve", lambda e: e.tensor_tensor(t2, s1, t1, ALU.mult),
             reads=[s1, t1], writes=[t2], tag="l2_s2a")
        P.op("dve", lambda e: e.tensor_scalar_add(t2, t2, C.T * 1e-8),
             reads=[t2], writes=[t2], tag="l2_s2")
        P.op("dve", lambda e: e.reciprocal(t2, t2),
             reads=[t2], writes=[t2], tag="l2_r2")
        P.op("dve", lambda e: e.tensor_mul(t1, t1, t2),
             reads=[t1, t2], writes=[t1], tag="l2_A")
        P.op("dve", lambda e: e.tensor_scalar_mul(t2, t2, 1e-8),
             reads=[t2], writes=[t2], tag="l2_B")
        abps = C.PAIR3[0:64, 0:512]
        ones32 = C.CONSTF[32:33, 0:64]
        P.op("pe", lambda e: e.matmul(abps, ones32, t1,
                                      start=True, stop=True),
             reads=[ones32, t1], writes=[abps], tag="l2_Ab")
        rb = C.SCR[0:64, 0:512]
        P.op("dve", lambda e: e.tensor_copy(rb, abps), reads=[abps],
             writes=[rb], tag="l2_Abev")
        yps = C.PAIR2[0:65, 512:1024]
        P.op("pe", lambda e: e.matmul(yps, vtrow, t2, start=True, stop=True),
             reads=[vtrow, t2], writes=[yps], tag="l2_Bmm")
        tmp = C.SCR[64:128, 0:512]
        P.op("dve", lambda e: e.tensor_mul(tmp, ctx[0:64, 0:512], rb),
             reads=[ctx[0:64, 0:512], rb], writes=[tmp], tag="l2_ctxA")
        fc = h // 2
        hh = h % 2
        dst = C.SCRB[hh * 64:(hh + 1) * 64,
                     CTXB_OFF + fc * 512:CTXB_OFF + (fc + 1) * 512]
        P.op("dve", lambda e: e.tensor_add(dst, tmp, yps[0:64, 0:512]),
             reads=[tmp, yps], writes=[dst], tag="l2_ctxB")

    blk.ctx_evac = ctx_evac
    blk.load_w = load_w
    blk.load_wo = load_wo
    attn_phase_A(C, blk)
    attn_phase_B(C, blk)


def emit_L3_attn(C, x_in, x_out, W):
    nc = C.nc
    P = C.P
    T = C.T
    blk = Blk()
    blk.variant = "l3"
    blk.causal = False
    blk.logit_scale = 1.0
    blk.x_in = x_in
    blk.x_res = x_in
    blk.x_out = x_out
    blk.q_evac_scale = 1.0
    blk.pre_ln_hook = None
    blk.v_bias = None
    blk.out_dchunks = (0, 2)
    blk.wo_off = 0
    blk.qscr = nc.dram_tensor("l3a_qscr", [D, C.T], BF16, kind="Internal").ap()
    blk.xfscr = nc.dram_tensor("l3a_xfscr", [D, C.T], BF16, kind="Internal").ap()
    knscr = nc.dram_tensor("l3a_kn", [H, C.T], F32, kind="Internal").ap()

    wfull = WSpec(C, "l3a_qkv", W["L3_attn_Wqkv"].ap(), D, 3 * D, "direct")
    wo = WSpec(C, "l3a_o", W["L3_attn_Wproj"].ap(), D, D, "direct")
    bq_flat = W["L3_attn_bqkv"].ap()
    blk.q_bias = C.STAT[:, 66:74]
    blk.k_bias = C.STAT[:, 74:82]
    load_bias_cols(C, bq_flat[1024:2048], blk.q_bias)
    load_bias_cols(C, bq_flat[0:1024], blk.k_bias)
    bvrow = C.SCR[0:1, 2048:3072]
    P.dma("qsp", bvrow, bass.AP(bq_flat.tensor, 2048, [[0, 1], [1, 1024]]),
          tag="l3a_bv")
    bpt = C.VECROW[:, 512:1536]
    bcast_vec_tile(C, W["L3_attn_bproj"], bpt)
    sig1 = C.STAT[0:1, 95:96]
    P.dma("qsp", sig1, W["L3_attn_r_sigma"].ap()[0:1], tag="l3a_sig")
    sigps = C.PS6[0:128, 0:512]
    P.op("pe", lambda e: e.matmul(sigps[:, 0:1], ones_row(C, 128), sig1,
                                  start=True, stop=True),
         reads=[sig1, ones_row(C, 128)], writes=[sigps], tag="l3a_sigb")
    negsig = C.STAT[:, 93:94]
    twosig = C.STAT[:, 94:95]
    P.op("act", lambda e: e.activation(negsig, sigps[:, 0:1], AF.Copy,
                                       scale=-1.0),
         reads=[sigps], writes=[negsig], tag="l3a_nsig")
    P.op("act", lambda e: e.activation(twosig, sigps[:, 0:1], AF.Copy,
                                       scale=2.0),
         reads=[sigps], writes=[twosig], tag="l3a_2sig")

    def load_w(C, which):
        col0 = {"q": D, "k": 0, "v": 2 * D}[which]
        for dc in range(NFC):
            wfull.load_slab(C.WBUF[:, dc * 1024:(dc + 1) * 1024],
                            dc * 128, (dc + 1) * 128, col0, col0 + D)

    def load_wo(C):
        for fc in range(NFC):
            wo.load_slab(C.WBUF[:, fc * 1024:(fc + 1) * 1024],
                         fc * 128, (fc + 1) * 128, 0, D)

    def _norms(C, fc, tb, stg, dst_kind):
        sq = C.SCR[:, 1024:1536]
        P.op("act", lambda e: e.activation(sq, stg, AF.Square),
             reads=[stg], writes=[sq], tag="l3a_sq")
        for hh in range(2):
            h = fc * 2 + hh
            ps = C.PS6[0:1, 0:512]
            lhsT = C.CONSTF[hh * 64:(hh + 1) * 64, 128:129]
            rhs = sq[hh * 64:(hh + 1) * 64, :]
            P.op("pe", lambda e, _l=lhsT, _r=rhs, _p=ps: e.matmul(
                _p, _l, _r, start=True, stop=True),
                reads=[lhsT, rhs], writes=[ps], tag="l3a_nmm")
            row = C.SCR[0:1, 1536:2048]
            P.op("dve", lambda e, _r=row, _p=ps: e.tensor_copy(_r, _p),
                 reads=[ps], writes=[row], tag="l3a_nev")
            if dst_kind == "kn":
                P.dma("qsp", knscr[h:h + 1, tb * 512:(tb + 1) * 512], row,
                      tag="l3a_knw")
            else:
                for tt in range(4):
                    jt = tb * 4 + tt
                    cps = C.PS6[0:128, 0:512]
                    rsl = row[0:1, tt * 128:(tt + 1) * 128]
                    one1 = ones_row(C, 1)
                    P.op("pe", lambda e, _c=cps, _r=rsl, _o=one1: e.matmul(
                        _c[:, 0:1], _r, _o, start=True, stop=True),
                        reads=[rsl, one1], writes=[cps], tag="l3a_qc")
                    dst = C.VECROW[:, h * 32 + (jt % 32):h * 32 + (jt % 32) + 1]
                    P.op("act", lambda e, _d=dst, _c=cps: e.activation(
                        _d, _c[:, 0:1], AF.Copy, scale=negsig),
                        reads=[cps, negsig], writes=[dst], tag="l3a_qcev")

    def qn_hook(C, blk, fc, tb, stg):
        _norms(C, fc, tb, stg, "kn")

    def kn_hook(C, blk, fc, tb, stg):
        _norms(C, fc, tb, stg, "qn")

    blk.qn_hook = qn_hook
    blk.kn_hook = kn_hook

    def l3_exp(C, blk, h, jt, ps, pt):
        qnc = C.VECROW[:, h * 32 + (jt % 32):h * 32 + (jt % 32) + 1]
        P.op("act", lambda e: e.activation(pt, ps, AF.Exp, bias=qnc,
                                           scale=twosig),
             reads=[ps, qnc, twosig], writes=[pt], tag="l3a_exp")

    blk.l3_exp = l3_exp

    def ctx_evac(C, blk, qb, h, ctx):
        srow = C.SCR[0:1, 1536:2048]
        P.op("dve", lambda e: e.tensor_copy(srow, ctx[64:65, 0:512]),
             reads=[ctx[64:65, 0:512]], writes=[srow], tag="l3a_srow")
        bvh = bvrow[0:1, h * 64:(h + 1) * 64]
        P.op("pe", lambda e: e.matmul(ctx[0:64, 0:512], bvh, srow,
                                      start=False, stop=True),
             reads=[bvh, srow], writes=[ctx], tag="l3a_rank1")
        knsl = C.SCR[0:1, 512:1024]
        P.dma("qsp", knsl, knscr[h:h + 1, qb * 512:(qb + 1) * 512],
              tag="l3a_knr")
        wrow = C.SCR[0:1, 3072:3584]
        P.op("act", lambda e: e.activation(wrow, knsl, AF.Exp,
                                           scale=C.STAT[0:1, 93:94]),
             reads=[knsl, C.STAT[0:1, 93:94]], writes=[wrow], tag="l3a_wrow")
        bc = C.PS6[0:64, 0:512]
        bcast_row(C, wrow, 64, bc)
        rb = C.SCR[0:64, 0:512]
        P.op("dve", lambda e: e.tensor_copy(rb, bc), reads=[bc], writes=[rb],
             tag="l3a_rbev")
        fc = h // 2
        hh = h % 2
        dst = C.SCRB[hh * 64:(hh + 1) * 64,
                     CTXB_OFF + fc * 512:CTXB_OFF + (fc + 1) * 512]
        P.op("dve", lambda e: e.tensor_mul(dst, ctx[0:64, 0:512], rb),
             reads=[ctx[0:64, 0:512], rb], writes=[dst], tag="l3a_ctxm")

    blk.ctx_evac = ctx_evac

    def residual(C, blk, qtg, pslist):
        xb = C.XBUF[:, 0:1024]
        P.dma("qsp", xb, blk.x_res[qtg * 128:(qtg + 1) * 128, :], tag="xres_l")
        S1 = C.SCR[:, 1024:2048]
        for i, ps in enumerate(pslist):
            sl = S1[:, i * 512:(i + 1) * 512]
            bp = bpt[:, i * 512:(i + 1) * 512]
            P.op("dve", lambda e, _s=sl, _p=ps, _b=bp: e.tensor_add(
                _s, _p, _b), reads=[ps, bp], writes=[sl], tag="l3a_bp")
            xs = xb[:, i * 512:(i + 1) * 512]
            P.op("dve", lambda e, _s=sl, _x=xs: e.tensor_add(_s, _s, _x),
                 reads=[sl, xs], writes=[sl], tag="l3a_rx")
        P.dma("qsp", blk.x_out[qtg * 128:(qtg + 1) * 128, :], S1, tag="xres_w")

    blk.load_w = load_w
    blk.load_wo = load_wo
    blk.residual_hook = residual
    attn_phase_A(C, blk)
    attn_phase_B(C, blk)



FP4_MID = [0.25, 0.75, 1.25, 1.75, 2.5, 3.5, 5.0]
FP4_DLT = [0.5, 0.5, 0.5, 0.5, 1.0, 1.0, 2.0]
L1I = 2816


def emit_quant(C, src_t, src_col, W, dst_bf16, tag):
    P = C.P
    nblk = W // 16
    rs = src_t.ap().ap[0][0]
    src = bass.AP(src_t, src_col, [[rs, 128], [1, W]])

    def v3(t, col, inner, n2, stride2):
        r = t.ap().ap[0][0]
        return bass.AP(t, col, [[r, 128], [stride2, n2], [1, inner]])

    sgn = C.SCRB[:, 8192:8192 + W]
    absz = C.SCR[:, 1024:1024 + W]
    acc = C.SCR[:, 2048:2048 + W]
    scrr = C.SCR.ap().ap[0][0]
    P.op("act", lambda e: e.activation(sgn, src, AF.Sign),
         reads=[src], writes=[sgn], tag=tag + "_sgn")
    P.op("act", lambda e: e.activation(absz, src, AF.Abs),
         reads=[src], writes=[absz], tag=tag + "_abs")
    a0 = v3(C.SCR, 1024, 8, nblk, 16)
    a1 = v3(C.SCR, 1024 + 8, 8, nblk, 16)
    t1 = v3(C.SCR, 3072, 8, nblk, 8)
    P.op("dve", lambda e: e.tensor_tensor(t1, a0, a1, ALU.max),
         reads=[absz], writes=[C.SCR[:, 3072:3072 + W // 2]], tag=tag + "_t1")
    t1a = v3(C.SCR, 3072, 4, nblk, 8)
    t1b = v3(C.SCR, 3076, 4, nblk, 8)
    t2 = v3(C.SCR, 3584, 4, nblk, 4)
    P.op("dve", lambda e: e.tensor_tensor(t2, t1a, t1b, ALU.max),
         reads=[C.SCR[:, 3072:3072 + W // 2]],
         writes=[C.SCR[:, 3584:3584 + W // 4]], tag=tag + "_t2")
    t2a = v3(C.SCR, 3584, 2, nblk, 4)
    t2b = v3(C.SCR, 3586, 2, nblk, 4)
    t3 = v3(C.SCR, 3840, 2, nblk, 2)
    P.op("dve", lambda e: e.tensor_tensor(t3, t2a, t2b, ALU.max),
         reads=[C.SCR[:, 3584:3584 + W // 4]],
         writes=[C.SCR[:, 3840:3840 + W // 8]], tag=tag + "_t3")
    t3a = v3(C.SCR, 3840, 1, nblk, 2)
    t3b = v3(C.SCR, 3841, 1, nblk, 2)
    bm = C.SCR[:, 3968:3968 + nblk]
    P.op("dve", lambda e: e.tensor_tensor(bm, t3a, t3b, ALU.max),
         reads=[C.SCR[:, 3840:3840 + W // 8]], writes=[bm], tag=tag + "_bm")
    P.op("dve", lambda e: e.tensor_scalar(bm, bm, 1.0 / 6.0, 1e-12,
                                          ALU.mult, ALU.max),
         reads=[bm], writes=[bm], tag=tag + "_scale")
    thr = C.SCR[:, 4032 - 64:4032 - 64 + nblk] if nblk <= 64 else None
    thr = C.SCR[:, 4032:4032 + nblk]
    absz3 = v3(C.SCR, 1024, 16, nblk, 16)
    acc3 = v3(C.SCR, 2048, 16, nblk, 16)
    cmp3 = v3(C.SCR, 0, 16, nblk, 16)
    cmp = C.SCR[:, 0:W]
    for k in range(7):
        P.op("dve", lambda e, _m=FP4_MID[k]: e.tensor_scalar_mul(thr, bm, _m),
             reads=[bm], writes=[thr], tag=tag + "_thr")
        thrb = v3(C.SCR, 4032, 1, nblk, 1)
        thrb = bass.AP(C.SCR, 4032, [[scrr, 128], [1, nblk], [0, 16]])
        P.op("dve", lambda e, _t=thrb: e.tensor_tensor(cmp3, absz3, _t,
                                                       ALU.is_gt),
             reads=[absz, thr], writes=[cmp], tag=tag + "_cmp")
        if k == 0:
            P.op("dve", lambda e, _d=FP4_DLT[k]: e.tensor_scalar_mul(
                acc, cmp, _d), reads=[cmp], writes=[acc], tag=tag + "_a0")
        else:
            P.op("dve", lambda e, _d=FP4_DLT[k]: e.scalar_tensor_tensor(
                acc, cmp, _d, acc, ALU.mult, ALU.add),
                reads=[cmp, acc], writes=[acc], tag=tag + "_ak")
    P.op("dve", lambda e: e.tensor_mul(acc, acc, sgn),
         reads=[acc, sgn], writes=[acc], tag=tag + "_sgnm")
    scaleb = bass.AP(C.SCR, 3968, [[scrr, 128], [1, nblk], [0, 16]])
    dst3 = bass.AP(dst_bf16.tensor, dst_bf16.offset,
                   [[dst_bf16.ap[0][0], 128], [16, nblk], [1, 16]])
    P.op("dve", lambda e: e.tensor_tensor(dst3, acc3, scaleb, ALU.mult),
         reads=[acc, bm], writes=[dst_bf16], tag=tag + "_q")


def emit_L1_ffn(C, x_in, x_out, W):
    nc = C.nc
    P = C.P
    T = C.T
    blk = Blk()
    blk.x_in = x_in
    blk.tile_hook = None

    wg_q = nc.dram_tensor("l1f_wgq", [L1I, D], BF16, kind="Internal").ap()
    wu_q = nc.dram_tensor("l1f_wuq", [L1I, D], BF16, kind="Internal").ap()
    wd_q = nc.dram_tensor("l1f_wdq", [D, L1I], BF16, kind="Internal").ap()
    specs = [(W["L1_ffn_w_gate"].ap(), wg_q, L1I, D),
             (W["L1_ffn_w_up"].ap(), wu_q, L1I, D),
             (W["L1_ffn_w_down"].ap(), wd_q, D, L1I)]
    for si, (wsrc, wdst, rows, cols) in enumerate(specs):
        r0 = 0
        while r0 < rows:
            r1 = min(r0 + 128, rows)
            c0 = 0
            while c0 < cols:
                c1 = min(c0 + 1024, cols)
                Wc = c1 - c0
                fsl = C.MASKQ[0:r1 - r0, 0:Wc]
                P.dma("qsp", fsl, wsrc[r0:r1, c0:c1], tag="l1f_wl")
                qd = C.SCRB[0:r1 - r0, 0:Wc]
                emit_quant(C, C.MASKQ, 0, Wc, qd, f"l1fw{si}")
                P.dma("qsp", wdst[r0:r1, c0:c1], qd, tag="l1f_ws")
                c0 = c1
            r0 = r1

    wgT = nc.dram_tensor("l1f_wgT", [D, L1I], BF16, kind="Internal").ap()
    wuT = nc.dram_tensor("l1f_wuT", [D, L1I], BF16, kind="Internal").ap()
    wdT = nc.dram_tensor("l1f_wdT", [L1I, D], BF16, kind="Internal").ap()
    for (src, dst, rows, cols) in ((wg_q, wgT, L1I, D), (wu_q, wuT, L1I, D),
                                   (wd_q, wdT, D, L1I)):
        stg = C.SCRB[:, STG_OFF:STG_OFF + 1024]
        for c0 in range(0, cols, 128):
            for r0 in range(0, rows, 1024):
                r1 = min(r0 + 1024, rows)
                P.dmat("qact", stg[:, 0:r1 - r0], src[r0:r1, c0:c0 + 128],
                       tag="l1f_pt")
                P.dma("qact", dst[c0:c0 + 128, r0:r1], stg[:, 0:r1 - r0],
                      tag="l1f_ptw")

    def pre_hook(C, blk, ti, xb, negmu, rstd, xn):
        zf = C.MASKQ[:, 0:1024]
        P.op("dve", lambda e: e.tensor_scalar(zf, xb, negmu, rstd,
                                              ALU.add, ALU.mult),
             reads=[xb, negmu, rstd], writes=[zf], tag="l1f_zf")
        emit_quant(C, C.MASKQ, 0, 1024, xn, "l1fx")

    blk.pre_ln_hook = pre_hook
    ffn_prep(C, blk)

    OCS = [512, 512, 512, 512, 512, 256]
    for dc in range(NFC):
        dst = C.BIGV[:, dc * L1I:(dc + 1) * L1I]
        P.dma("qact", dst, wuT[dc * 128:(dc + 1) * 128, 0:L1I], tag="l1f_wuc")
    for ti in range(C.NT):
        co = 0
        for oc, ow in enumerate(OCS):
            for which, wqd in (("g", wg_q), ("u", wu_q)):
                if which == "g":
                    for dc in range(NFC):
                        dst = C.WBUF[:, dc * 512:dc * 512 + ow]
                        P.dma("qact", dst,
                              wgT[dc * 128:(dc + 1) * 128, co:co + ow],
                              tag="l1f_wg")
                ps = (C.PAIR0 if which == "g" else C.PAIR1)[:, 0:ow]
                for dc in range(NFC):
                    lhsT = C.BIGK[:, dc * T + ti * 128:dc * T + (ti + 1) * 128]
                    rhs = (C.WBUF[:, dc * 512:dc * 512 + ow] if which == "g"
                           else C.BIGV[:, dc * L1I + co:dc * L1I + co + ow])
                    P.op("pe", lambda e, _o=ps, _l=lhsT, _r=rhs, _s=(dc == 0),
                         _e=(dc == NFC - 1): e.matmul(_o, _l, _r, start=_s,
                                                      stop=_e),
                         reads=[lhsT, rhs], writes=[ps], tag="l1f_" + which)
                if which == "g":
                    gs = C.MASKQ[:, 0:ow]
                    P.op("act", lambda e, _g=gs, _p=ps: e.activation(
                        _g, _p, AF.Silu), reads=[ps], writes=[gs],
                        tag="l1f_silu")
                else:
                    gs = C.MASKQ[:, 0:ow]
                    P.op("dve", lambda e, _g=gs, _p=ps: e.tensor_mul(
                        _g, _g, _p), reads=[gs, ps], writes=[gs],
                        tag="l1f_prod")
            hq = C.SCRB[:, co:co + ow]
            emit_quant(C, C.MASKQ, 0, ow, hq, "l1fh")
            co += ow
        for hc in range(22):
            pt = C.PAIR2[:, (hc % 2) * 512:(hc % 2) * 512 + 64].bitcast(BF16)
            s = C.SCRB[:, hc * 128:(hc + 1) * 128]
            P.op("pe", lambda e, _p=pt, _s=s: e.transpose(_p, _s, C.IDB[:, :]),
                 reads=[s, C.IDB[:, :]], writes=[pt], tag="l1f_t")
            dst = C.SCRB[:, 4096 + hc * 128:4096 + (hc + 1) * 128]
            P.op("dve", lambda e, _d=dst, _p=pt: e.tensor_copy(_d, _p),
                 reads=[pt], writes=[dst], tag="l1f_tev")
        for hc in range(22):
            wb = C.WBUF[:, 8192 + (hc % 2) * 1024:8192 + (hc % 2) * 1024 + 1024]
            P.dma("qact", wb, wdT[hc * 128:(hc + 1) * 128, 0:D], tag="l1f_wd")
            lhsT = C.SCRB[:, 4096 + hc * 128:4096 + (hc + 1) * 128]
            for half in range(2):
                ps = C.PAIR3[:, half * 512:(half + 1) * 512]
                rhs = wb[:, half * 512:(half + 1) * 512]
                P.op("pe", lambda e, _o=ps, _l=lhsT, _r=rhs, _s=(hc == 0),
                     _e=(hc == 21): e.matmul(_o, _l, _r, start=_s, stop=_e),
                     reads=[lhsT, rhs], writes=[ps], tag="l1f_dn")
        xb = C.XBUF[:, 0:1024]
        P.dma("qsp", xb, x_in[ti * 128:(ti + 1) * 128, :], tag="l1f_xl")
        S1 = C.SCR[:, 1024:2048]
        for half in range(2):
            ps = C.PAIR3[:, half * 512:(half + 1) * 512]
            sl = S1[:, half * 512:(half + 1) * 512]
            xs = xb[:, half * 512:(half + 1) * 512]
            P.op("dve", lambda e, _s=sl, _p=ps, _x=xs: e.tensor_add(
                _s, _p, _x), reads=[ps, xs], writes=[sl], tag="l1f_res")
        P.dma("qsp", x_out[ti * 128:(ti + 1) * 128, :], S1, tag="l1f_xw")



_CACHE = {}


def build_program(T, blocks):
    nc = bass.Bass("TRN2", target_bir_lowering=False)
    es = contextlib.ExitStack()
    x_ext = nc.dram_tensor("x", [T, D], F32, kind="ExternalInput")
    y_ext = nc.dram_tensor("y", [T, D], F32, kind="ExternalOutput")
    wt = {}
    from collections import OrderedDict
    SPEC = _input_specs()
    for name, shape in SPEC.items():
        if name == "L0_attn_x":
            continue
        wt[name] = nc.dram_tensor(name, list(shape) if shape else [1], F32,
                                  kind="ExternalInput")
    P = Prog(nc)
    C = mk_ctx(nc, P, T)
    emit_consts(C)
    xres = nc.dram_tensor("xres", [T, D], F32, kind="Internal").ap()
    chain = []
    cur_in = x_ext.ap()
    n = len(blocks)
    for i, bname in enumerate(blocks):
        out = y_ext.ap() if i == n - 1 else xres
        if bname == "L0a":
            emit_L0_attn(C, cur_in, out, wt)
        elif bname == "L1a":
            emit_L1_attn(C, cur_in, out, wt)
        elif bname == "L0f":
            emit_L0_ffn(C, cur_in, out, wt)
        elif bname == "L2a":
            emit_L2_attn(C, cur_in, out, wt)
        elif bname == "L3a":
            emit_L3_attn(C, cur_in, out, wt)
        elif bname == "L1f":
            emit_L1_ffn(C, cur_in, out, wt)
        elif bname == "L2f":
            emit_L2_ffn(C, cur_in, out, wt)
        elif bname == "L3f":
            emit_L3_ffn(C, cur_in, out, wt)
        else:
            raise ValueError(bname)
        cur_in = out
    P.build(es)
    return nc


def _input_specs():
    return {
        "L0_attn_x": (8, 4096, 1024),
        "L0_attn_P_q": (16, 1024, 64), "L0_attn_P_k": (16, 1024, 64),
        "L0_attn_P_v": (16, 1024, 64), "L0_attn_P_o": (16, 1024, 64),
        "L0_ffn_n1_w": (1024,), "L0_ffn_n2_w": (1024,),
        "L0_ffn_wq": (1024, 1024), "L0_ffn_wk": (1024, 1024),
        "L0_ffn_wv": (1024, 1024), "L0_ffn_wo": (1024, 1024),
        "L0_ffn_qn_w": (64,), "L0_ffn_kn_w": (64,),
        "L0_ffn_gate_w": (4, 1024),
        "L0_ffn_e0_up": (2048, 1024), "L0_ffn_e0_down": (1024, 2048),
        "L0_ffn_e1_w1": (2048, 1024), "L0_ffn_e1_w2": (2048, 1024),
        "L0_ffn_e1_down": (1024, 2048),
        "L0_ffn_e2_l1": (2048, 1024), "L0_ffn_e2_l2": (2048, 2048),
        "L0_ffn_e2_l3": (2048, 2048), "L0_ffn_e2_l4": (1024, 2048),
        "L0_ffn_e3_down": (1024, 1024), "L0_ffn_e3_up": (2048, 1024),
        "L0_ffn_e3_out": (1024, 2048),
        "L1_attn_Wq": (1024, 16, 64), "L1_attn_bq": (16, 64),
        "L1_attn_Wk": (1024, 16, 64), "L1_attn_bk": (16, 64),
        "L1_attn_Wv": (1024, 16, 64), "L1_attn_bv": (16, 64),
        "L1_attn_Wo": (16, 64, 1024), "L1_attn_bo": (1024,),
        "L1_attn_gamma": (1024,), "L1_attn_beta": (1024,),
        "L1_ffn_w_gate": (2816, 1024), "L1_ffn_w_up": (2816, 1024),
        "L1_ffn_w_down": (1024, 2816),
        "L2_attn_w_qkv": (3072, 1024), "L2_attn_w_proj": (1024, 1024),
        "L2_ffn_gate_w": (8, 1024), "L2_ffn_expert_w": (8, 1024, 1024),
        "L3_attn_Wqkv": (1024, 3072), "L3_attn_bqkv": (3072,),
        "L3_attn_r_sigma": (), "L3_attn_Wproj": (1024, 1024),
        "L3_attn_bproj": (1024,),
        "L3_ffn_router_w": (8, 1024), "L3_ffn_w1": (1024, 12288),
        "L3_ffn_w2": (12288, 1024),
    }


ALL_BLOCKS = ["L0a", "L0f", "L1a", "L1f", "L2a", "L2f", "L3a", "L3f"]


def kernel(**inputs):
    T = 4096
    key = ("full", T, tuple(ALL_BLOCKS))
    if key not in _CACHE:
        _CACHE[key] = build_program(T, ALL_BLOCKS)
    nc = _CACHE[key]
    x = np.asarray(inputs["L0_attn_x"])
    in_maps = []
    for b in range(8):
        m = {"x": np.ascontiguousarray(x[b])}
        for name in _input_specs():
            if name == "L0_attn_x":
                continue
            v = np.asarray(inputs[name], dtype=np.float32)
            if v.ndim == 0:
                v = v.reshape(1)
            m[name] = np.ascontiguousarray(v)
        in_maps.append(m)
    res = run_bass_kernel_spmd(nc, in_maps, core_ids=list(range(8)))
    return np.stack([r["y"] for r in res.results], axis=0)
